# version 43
# baseline (speedup 1.0000x reference)
import contextlib
import numpy as np
import ml_dtypes
import concourse.bass as bass
import concourse.mybir as mybir
from concourse.bass_utils import run_bass_kernel_spmd

F32 = mybir.dt.float32
BF16 = mybir.dt.bfloat16
Alu = mybir.AluOpType
Act = mybir.ActivationFunctionType
Ax = mybir.AxisListType

C, T, H, NCH = 1024, 2048, 128, 8
TB1, NTB1 = 512, 4
QB, NQB = 512, 4
TBF, NTBF = 512, 4
KJ = 16
SCALE = float(np.float32(128.0) ** np.float32(-0.5))
EPS = 1e-5
ARENA_W = 52736

ENGS = ("sync", "scalar", "vector", "tensor", "gpsimd")
NQSLOT = 10


class Buf:
    __slots__ = ("name", "writer", "readers")

    def __init__(self, name):
        self.name = name
        self.writer = None
        self.readers = {}


class Op:
    __slots__ = ("eng", "seq", "fn", "is_dma", "deps", "needs_inc", "inc_val",
                 "tag", "qslot")


class Tracker:

    def __init__(self):
        self.ops = {e: [] for e in ENGS}
        self.last_real = {e: None for e in ENGS}
        self.dma_cnt = {e: 0 for e in ENGS}
        self.slot_last = {}

    def op(self, eng, fn, reads=(), writes=(), is_dma=False, tag=""):
        o = Op()
        o.eng, o.fn, o.is_dma, o.tag = eng, fn, is_dma, tag
        o.seq = len(self.ops[eng])
        o.needs_inc = False
        o.inc_val = None
        o.qslot = None
        if is_dma:
            idx = self.dma_cnt[eng]
            self.dma_cnt[eng] = idx + 1
            o.qslot = idx % NQSLOT
            o.inc_val = (idx // NQSLOT + 1) * 16
            o.needs_inc = True
        dep = set()
        for b in reads:
            w = b.writer
            if w is not None and (w.is_dma or w.eng != eng):
                dep.add(w)
            b.readers[(eng, is_dma, o.qslot)] = o
        for b in writes:
            w = b.writer
            if w is not None and (w.is_dma or w.eng != eng):
                dep.add(w)
            for r in b.readers.values():
                if r is not o and (r.is_dma or r.eng != eng):
                    dep.add(r)
            b.writer = o
            b.readers = {}
        if is_dma:
            prev = self.slot_last.get((eng, o.qslot))
            if prev is not None:
                dep.add(prev)
            self.slot_last[(eng, o.qslot)] = o
        o.deps = dep
        self.ops[eng].append(o)
        if fn is not None:
            self.last_real[eng] = o
        return o

    def barrier(self):
        lasts = [self.last_real[e] for e in ENGS if self.last_real[e] is not None]
        lasts += list(self.slot_last.values())
        for e in ENGS:
            o = Op()
            o.eng, o.fn, o.is_dma, o.tag = e, None, False, "barrier"
            o.seq = len(self.ops[e])
            o.needs_inc = False
            o.inc_val = None
            o.qslot = None
            o.deps = {l for l in lasts if l.eng != e or l.is_dma}
            self.ops[e].append(o)

    def resolve(self):
        for e in ENGS:
            for o in self.ops[e]:
                for d in o.deps:
                    d.needs_inc = True
        for e in ENGS:
            cnt = 0
            for o in self.ops[e]:
                if o.is_dma:
                    continue
                if o.needs_inc:
                    if o.fn is None:
                        raise RuntimeError(f"dep on pure-wait op {o.eng}#{o.seq}")
                    cnt += 1
                    o.inc_val = cnt
        return self

    def emit(self, block, sems):
        def mk(section_name):
            ops = self.ops[section_name]

            def body(engine):
                covered = {}
                for o in ops:
                    for d in sorted(o.deps,
                                    key=lambda d: (d.eng, d.is_dma, d.qslot or 0,
                                                   d.inc_val)):
                        key = (("q", d.eng, d.qslot) if d.is_dma else ("c", d.eng))
                        v = d.inc_val
                        assert v is not None, (d.eng, d.seq, d.tag)
                        if covered.get(key, 0) < v:
                            engine.wait_ge(sems[key], v)
                            covered[key] = v
                    if o.fn is None:
                        continue
                    inst = o.fn(engine)
                    if o.needs_inc:
                        key = (("q", o.eng, o.qslot) if o.is_dma else ("c", o.eng))
                        inst.then_inc(sems[key], 16 if o.is_dma else 1)
            return body

        for e in ENGS:
            getattr(block, e)(mk(e))


class View:
    __slots__ = ("ap", "buf", "off", "words")

    def __getitem__(self, idx):
        return self.ap[idx]


class Arena:
    def __init__(self, tensor, words):
        self.t = tensor
        self.free_list = [(0, words)]

    def alloc(self, name, shape, dtype):
        per_part = 1
        for s in shape[1:]:
            per_part *= s
        w = per_part if dtype == F32 else (per_part + 1) // 2
        w = (w + 1) // 2 * 2
        for i, (off, sz) in enumerate(self.free_list):
            if sz >= w:
                if sz == w:
                    self.free_list.pop(i)
                else:
                    self.free_list[i] = (off + w, sz - w)
                v = View()
                v.off, v.words = off, w
                ap = self.t[0:shape[0], off:off + w]
                if dtype == BF16:
                    ap = ap.bitcast(BF16)
                n_free = per_part
                if len(shape) == 2:
                    ap = ap[:, 0:shape[1]]
                elif len(shape) == 3:
                    ap = ap[:, 0:n_free].rearrange("p (a b) -> p a b", a=shape[1])
                else:
                    raise ValueError(shape)
                v.ap = ap
                v.buf = Buf(name)
                return v
        raise MemoryError(f"arena OOM for {name} ({w} words); free={self.free_list}")

    def free(self, v):
        fl = self.free_list
        fl.append((v.off, v.words))
        fl.sort()
        out = []
        for off, sz in fl:
            if out and out[-1][0] + out[-1][1] == off:
                out[-1] = (out[-1][0], out[-1][1] + sz)
            else:
                out.append((off, sz))
        self.free_list = out


def build_program(debug=False):
    nc = bass.Bass("TRN2", target_bir_lowering=False)

    xT = nc.dram_tensor("xT", [C, T], F32, kind="ExternalInput").ap()
    wqT = nc.dram_tensor("wqT", [C, H], F32, kind="ExternalInput").ap()
    wkT = nc.dram_tensor("wkT", [C, H], F32, kind="ExternalInput").ap()
    wvT = nc.dram_tensor("wvT", [C, H], F32, kind="ExternalInput").ap()
    bqkv_d = nc.dram_tensor("bqkv", [H, 3], F32, kind="ExternalInput").ap()
    woT = nc.dram_tensor("woT", [H, C], F32, kind="ExternalInput").ap()
    bo_d = nc.dram_tensor("bo", [C], F32, kind="ExternalInput").ap()
    frT = nc.dram_tensor("frT", [C, C], F32, kind="ExternalInput").ap()
    fkT = nc.dram_tensor("fkT", [C, C], F32, kind="ExternalInput").ap()
    fvT = nc.dram_tensor("fvT", [C, C], F32, kind="ExternalInput").ap()
    mur_d = nc.dram_tensor("mur", [C], F32, kind="ExternalInput").ap()
    muk_d = nc.dram_tensor("muk", [C], F32, kind="ExternalInput").ap()
    tri_d = nc.dram_tensor("tri", [H, H], BF16, kind="ExternalInput").ap()
    ident_d = nc.dram_tensor("ident", [H, H], F32, kind="ExternalInput").ap()
    yT = nc.dram_tensor("yT", [C, T], F32, kind="ExternalOutput").ap()

    scr = {}
    scr_b = {}
    for nm in ("r", "k", "v"):
        scr[nm] = (nc.dram_tensor(f"s_hi_{nm}", [C, C], BF16).ap(),
                   nc.dram_tensor(f"s_lo_{nm}", [C, C], BF16).ap())
        scr_b[nm] = (Buf(f"s_hi_{nm}"), Buf(f"s_lo_{nm}"))
    x2d = nc.dram_tensor("x2d", [C, T], F32).ap()
    x2d_b2 = [[Buf(f"x2d{q}c{c}") for c in range(NCH)] for q in range(4)]
    x2d_r = x2d.rearrange("(c p) t -> p c t", p=128)

    dbg = {}
    if debug:
        for nm, shp in (("d_qhi", [H, T]), ("d_qlo", [H, T]), ("d_khi", [H, T]),
                        ("d_klo", [H, T]), ("d_vf", [H, T]), ("d_mil", [1, QB]),
                        ("d_ohi", [H, T]), ("d_olo", [H, T]), ("d_x2", [C, T]),
                        ("d_r2", [1, T]), ("d_mr2", [1, T]), ("d_rg", [C, TBF]),
                        ("d_ehi", [H, KJ * QB]), ("d_elo", [H, KJ * QB]),
                        ("d_kch", [H, NCH * TBF]), ("d_kcl", [H, NCH * TBF]),
                        ("d_xkh", [H, NCH * TBF]), ("d_xkl", [H, NCH * TBF])):
            d = BF16 if nm in ("d_qhi", "d_qlo", "d_khi", "d_klo", "d_ohi",
                               "d_olo", "d_ehi", "d_elo", "d_kch", "d_kcl",
                               "d_xkh", "d_xkl") else F32
            dbg[nm] = nc.dram_tensor(nm, shp, d, kind="ExternalOutput").ap()

    tr = Tracker()
    xT_r = xT.rearrange("(c p) t -> p c t", p=128)
    yT_r = yT.rearrange("(c p) t -> p c t", p=128)

    with (
        nc.sbuf_tensor("arena", [128, ARENA_W], F32) as arena_t,
        nc.psum_tensor("pm0", [128, 512], F32) as pm0_t,
        nc.psum_tensor("pm1", [128, 512], F32) as pm1_t,
        nc.psum_tensor("pa0", [128, 512], F32) as pa0_t,
        nc.psum_tensor("pa1", [128, 512], F32) as pa1_t,
        nc.psum_tensor("prow0", [128, 512], F32) as prow0_t,
        nc.psum_tensor("prow1", [128, 512], F32) as prow1_t,
        nc.psum_tensor("pbc0", [128, 512], F32) as pbc0_t,
        nc.psum_tensor("pbc1", [128, 512], F32) as pbc1_t,
        contextlib.ExitStack() as _es,
        nc.Block() as block,
    ):
        sems = {}
        for e in ("sync", "scalar", "gpsimd"):
            for s in range(NQSLOT):
                sems[("q", e, s)] = _es.enter_context(nc.semaphore(f"q_{e}_{s}"))
        for e in ("scalar", "vector", "tensor", "gpsimd"):
            sems[("c", e)] = _es.enter_context(nc.semaphore(f"c_{e}"))

        A = Arena(arena_t, ARENA_W)
        psum = {}
        for nm, t in (("pm0", pm0_t), ("pm1", pm1_t), ("pa0", pa0_t), ("pa1", pa1_t),
                      ("prow0", prow0_t), ("prow1", prow1_t), ("pbc0", pbc0_t),
                      ("pbc1", pbc1_t)):
            v = View()
            v.ap, v.buf, v.off, v.words = t[:], Buf(nm), -1, 0
            psum[nm] = v

        x_ring = [A.alloc(f"x_ring{i}", (128, NCH, 512), F32) for i in range(3)]
        x_ring_b = [[Buf(f"x{i}c{c}") for c in range(NCH)] for i in range(3)]
        w_prj = {}
        for nm in ("q", "k", "v"):
            w_prj[nm] = (A.alloc(f"w{nm}_hi", (128, NCH, H), BF16),
                         A.alloc(f"w{nm}_lo", (128, NCH, H), BF16))
        wo_hi = A.alloc("wo_hi", (128, C), BF16)
        wo_lo = A.alloc("wo_lo", (128, C), BF16)
        bqkv = A.alloc("bqkv", (128, 4), F32)
        bo_s = A.alloc("bo_s", (128, NCH), F32)
        mur_s = A.alloc("mur_s", (128, NCH), F32)
        muk_s = A.alloc("muk_s", (128, NCH), F32)
        q_ri = [(A.alloc(f"qh{i}", (128, 512), BF16),
                 A.alloc(f"ql{i}", (128, 512), BF16)) for i in range(2)]
        k_hi = A.alloc("k_hi", (128, T), BF16)
        k_lo = A.alloc("k_lo", (128, T), BF16)
        v_f = A.alloc("v_f", (128, 512), F32)
        vt_hi = A.alloc("vt_hi", (128, T), BF16)
        vt_lo = A.alloc("vt_lo", (128, T), BF16)
        o_hi = A.alloc("o_hi", (128, 512), BF16)
        o_lo = A.alloc("o_lo", (128, 512), BF16)
        r2row = A.alloc("r2row", (1, T), F32)
        mr2row = A.alloc("mr2row", (1, T), F32)
        ERING = 6
        e_hi = A.alloc("e_hi", (128, ERING, QB), BF16)
        e_lo = A.alloc("e_lo", (128, ERING, QB), BF16)
        e_hi_b = [Buf(f"e_hi{j}") for j in range(ERING)]
        e_lo_b = [Buf(f"e_lo{j}") for j in range(ERING)]
        ef32 = [A.alloc(f"ef32{i}", (128, 512), F32) for i in range(3)]
        of_blk = A.alloc("of_blk", (128, 512), F32)
        rsbc = A.alloc("rsbc", (128, 512), F32)
        fst = [A.alloc(f"fst{i}", (128, C), F32) for i in range(2)]
        fsh = [A.alloc("fsh0", (128, C), BF16)]
        fsl = [A.alloc("fsl0", (128, C), BF16)]
        ident_s = A.alloc("ident_s", (128, H), F32)
        tri_s = A.alloc("tri_s", (128, H), BF16)
        ones_k1 = A.alloc("ones_k1", (1, 128), BF16)
        ones_k1f = A.alloc("ones_k1f", (1, 128), F32)
        ones_m1 = A.alloc("ones_m1", (128, 2), BF16)
        ones_sq = A.alloc("ones_sq", (128, H), BF16)
        ones_m1f = A.alloc("ones_m1f", (128, 2), F32)
        rowA = A.alloc("rowA", (1, 512), F32)
        rowB = A.alloc("rowB", (1, 512), F32)
        rowR = A.alloc("rowR", (1, 512), F32)
        rowMR = A.alloc("rowMR", (1, 512), F32)
        stmp = A.alloc("stmp", (1, 512), F32)
        rowA2 = A.alloc("rowA2", (1, 512), F32)
        rowB2 = A.alloc("rowB2", (1, 512), F32)
        rowN2 = A.alloc("rowN2", (1, 512), F32)
        rowN1 = A.alloc("rowN1", (1, 512), F32)
        rs_row = A.alloc("rs_row", (1, 512), F32)

        x2tb = A.alloc("x2tb", (128, NCH, 512), F32)
        x2tb_b = [Buf(f"x2tb{c}") for c in range(NCH)]
        xs_hi = A.alloc("xs_hi", (128, NCH, 512), BF16)
        xs_lo = A.alloc("xs_lo", (128, NCH, 512), BF16)
        bc0 = A.alloc("bc0", (128, 512), F32)
        qstage = A.alloc("qstage", (128, 512), F32)
        kstage = A.alloc("kstage", (128, 512), F32)
        sq = [A.alloc(f"sq{i}", (128, 512), F32) for i in range(2)]
        vstg = [A.alloc(f"vstg{i}", (128, H), F32) for i in range(2)]
        wsumn = A.alloc("wsumn", (1, 3 * H), F32)

        def dma(eng, out_ap, in_ap, reads, writes, tag="", slow=False):
            tr.op(eng, (lambda o, i, s: lambda e: e.dma_start(
                o, i, allow_slow_non_contiguous=s))(out_ap, in_ap, slow),
                  reads=reads, writes=writes, is_dma=True, tag=tag)

        def mm(out_ap, l_ap, r_ap, reads, writes, start, stop, tag=""):
            tr.op("tensor",
                  (lambda o, l, r, st, sp: lambda e: e.matmul(o, l, r, start=st, stop=sp))
                  (out_ap, l_ap, r_ap, start, stop),
                  reads=reads, writes=writes, tag=tag)

        def act(out_ap, in_ap, func, reads, writes, bias=None, scale=None, tag=""):
            def f(o, i, fu, b, s):
                def run(e):
                    kw = {}
                    if b is not None:
                        kw["bias"] = b
                    if s is not None:
                        kw["scale"] = s
                    return e.activation(o, i, fu, **kw)
                return run
            tr.op("scalar", f(out_ap, in_ap, func, bias, scale),
                  reads=reads, writes=writes, tag=tag)

        def vop(fn, reads, writes, tag="", eng="vector"):
            tr.op(eng, fn, reads=reads, writes=writes, tag=tag)

        def t_sub(o, a, b):
            return lambda e: e.tensor_sub(o, a, b)

        def t_mul(o, a, b):
            return lambda e: e.tensor_mul(o, a, b)

        def t_add(o, a, b):
            return lambda e: e.tensor_add(o, a, b)

        def t_copy(o, a):
            return lambda e: e.tensor_copy(o, a)

        for tb in range(3):
            ts = tb * TB1
            for ch in range(NCH):
                dma("sync", x_ring[tb].ap[:, ch, :], xT_r[:, ch, ts:ts + 512],
                    [x_ring_b[tb][ch]], [x_ring_b[tb][ch]], f"x_in{tb}_{ch}")
        dma("sync", tri_s.ap, tri_d, [], [tri_s.buf])
        dma("sync", ident_s.ap, ident_d, [], [ident_s.buf])
        dma("sync", bqkv.ap[:, 0:3], bqkv_d, [], [bqkv.buf])
        dma("sync", bo_s.ap, bo_d.rearrange("(c p) -> p c", p=128), [], [bo_s.buf],
            slow=True)
        dma("sync", mur_s.ap, mur_d.rearrange("(c p) -> p c", p=128), [], [mur_s.buf],
            slow=True)
        dma("sync", muk_s.ap, muk_d.rearrange("(c p) -> p c", p=128), [], [muk_s.buf],
            slow=True)
        for v, val in ((ones_k1, 1.0), (ones_k1f, 1.0), (ones_m1, 1.0),
                       (ones_m1f, 1.0), (ones_sq, 1.0)):
            vop((lambda a, c_: lambda e: e.memset(a, c_))(v.ap, val), [], [v.buf],
                "memset")

        fst_hb = [[Buf(f"fst{i}h{h}") for h in range(2)] for i in range(2)]
        w_half = {"q": [(fst[0].ap[:, 0:512], fst_hb[0][0]),
                        (fst[0].ap[:, 512:1024], fst_hb[0][1])],
                  "k": [(fst[1].ap[:, 0:512], fst_hb[1][0]),
                        (fst[1].ap[:, 512:1024], fst_hb[1][1])],
                  "v": [(qstage.ap, qstage.buf), (kstage.ap, kstage.buf)]}
        for nm, srcd in (("q", wqT), ("k", wkT), ("v", wvT)):
            for h, (hap, hb) in enumerate(w_half[nm]):
                ap3 = hap.rearrange("p (a b) -> p a b", a=4)
                dma("sync", ap3,
                    srcd.rearrange("(c p) h -> p c h", p=128)[:, h * 4:h * 4 + 4, :],
                    [hb], [hb], f"w{nm}_in{h}")
        def emit_wsplits():
            for nm in ("q", "k", "v"):
                hi, lo = w_prj[nm]
                for h, (hap, hb) in enumerate(w_half[nm]):
                    ap3 = hap.rearrange("p (a b) -> p a b", a=4)
                    cs = slice(h * 4, h * 4 + 4)
                    act(hi.ap[:, cs, :], ap3, Act.Copy, [hb], [hi.buf],
                        tag=f"w{nm}_hi{h}")
                    vop(t_sub(lo.ap[:, cs, :], ap3, hi.ap[:, cs, :]),
                        [hb, hi.buf], [lo.buf], f"w{nm}_lo{h}")
            wstg1v = fst[0].ap
            dma("sync", wstg1v, woT, fst_hb[0], fst_hb[0], "wo_in")
            act(wo_hi.ap, wstg1v, Act.Copy, fst_hb[0], [wo_hi.buf])
            vop(t_sub(wo_lo.ap, wstg1v, wo_hi.ap), fst_hb[0] + [wo_hi.buf],
                [wo_lo.buf])
        def emit_wsums():
            for pj, nm in enumerate(("q", "k", "v")):
                w_h, w_l = w_prj[nm]
                i_mm = 0
                for wv in (w_h, w_l):
                    for ch in range(NCH):
                        mm(psum["pm0"].ap[0:1, 0:H], ones_m1.ap[:, 0:1],
                           wv.ap[:, ch, :], [ones_m1.buf, wv.buf],
                           [psum["pm0"].buf], i_mm == 0, i_mm == 15,
                           f"wsum_{nm}")
                        i_mm += 1
                act(wsumn.ap[0:1, pj * H:(pj + 1) * H], psum["pm0"].ap[0:1, 0:H],
                    Act.Identity, [psum["pm0"].buf], [wsumn.buf], scale=-1.0,
                    tag=f"wsumn_{nm}")

        slabs = []
        for nm, src in (("r", frT), ("k", fkT), ("v", fvT)):
            for ch in range(NCH):
                slabs.append((nm, src, ch))
        slab_i = [0]

        def stats_mm(src3, src_bufs, p0, p1, tag):
            for ch in range(NCH):
                mm(p0.ap[0:1, :], ones_m1f.ap[:, 0:1], src3[:, ch, :],
                   [ones_m1f.buf, src_bufs[ch]], [p0.buf], ch == 0, ch == NCH - 1,
                   f"{tag}_sum{ch}")
            for ch in range(NCH):
                s = sq[ch % 2]
                act(s.ap, src3[:, ch, :], Act.Square, [src_bufs[ch]], [s.buf],
                    tag=f"{tag}_sq{ch}")
                mm(p1.ap[0:1, :], ones_m1f.ap[:, 0:1], s.ap,
                   [ones_m1f.buf, s.buf], [p1.buf], ch == 0, ch == NCH - 1,
                   f"{tag}_msq{ch}")

        def ln_stats_rows(p0, p1, r_ap, r_buf, mr_ap, mr_buf, tag, scratch):
            ra, rb, rn = scratch
            vop((lambda o, i: lambda e: e.tensor_scalar_mul(o, i, 1.0 / C))
                (ra.ap, p0.ap[0:1, :]), [p0.buf], [ra.buf], f"{tag}_mean")
            vop((lambda o, i: lambda e: e.tensor_scalar_mul(o, i, 1.0 / C))
                (rb.ap, p1.ap[0:1, :]), [p1.buf], [rb.buf], f"{tag}_msq")
            vop((lambda o, a, b_: lambda e: e.scalar_tensor_tensor(
                o, a, -1.0, b_, op0=Alu.mult, op1=Alu.mult))
                (rn.ap, ra.ap, ra.ap), [ra.buf], [rn.buf], f"{tag}_negm2")
            vop((lambda o, a, b_: lambda e: e.scalar_tensor_tensor(
                o, a, EPS, b_, op0=Alu.add, op1=Alu.add))
                (rb.ap, rb.ap, rn.ap), [rb.buf, rn.buf], [rb.buf], f"{tag}_vareps")
            act(rb.ap, rb.ap, Act.Ln, [rb.buf], [rb.buf], tag=f"{tag}_ln")
            act(r_ap, rb.ap, Act.Exp, [rb.buf], [r_buf], scale=-0.5,
                tag=f"{tag}_rstd")
            vop(t_mul(mr_ap, ra.ap, r_ap), [ra.buf, r_buf], [mr_buf], f"{tag}_mr")

        def bcast_pair(r_ap, r_buf, mr_ap, mr_buf, b0v, b1v, n, tag):
            mm(psum["pbc0"].ap[:, 0:n], ones_k1f.ap, r_ap, [ones_k1f.buf, r_buf],
               [psum["pbc0"].buf], True, True, f"{tag}_bcR")
            mm(psum["pbc1"].ap[:, 0:n], ones_k1f.ap, mr_ap, [ones_k1f.buf, mr_buf],
               [psum["pbc1"].buf], True, True, f"{tag}_bcMR")
            vop(t_copy(b0v.ap[:, 0:n], psum["pbc0"].ap[:, 0:n]), [psum["pbc0"].buf],
                [b0v.buf])
            vop(t_copy(b1v.ap[:, 0:n], psum["pbc1"].ap[:, 0:n]), [psum["pbc1"].buf],
                [b1v.buf])

        def p1_rows(tb):
            ln_stats_rows(psum["prow0"], psum["prow1"], rowR.ap, rowR.buf,
                          rowMR.ap, rowMR.buf, f"ln1_{tb}", (rowA, rowB, rowN1))

        def p1_bcast(tb):
            mm(psum["pbc0"].ap, ones_k1f.ap, rowR.ap, [ones_k1f.buf, rowR.buf],
               [psum["pbc0"].buf], True, True, f"bcR_{tb}")
            vop(t_copy(bc0.ap, psum["pbc0"].ap), [psum["pbc0"].buf], [bc0.buf],
                f"bcRev_{tb}")

        def xsplit(tb):
            x3 = x_ring[tb % 3]
            x3b = x_ring_b[tb % 3]
            act(xs_hi.ap, x3.ap, Act.Copy, x3b, [xs_hi.buf], tag=f"xhi_{tb}")
            vop(t_sub(xs_lo.ap, x3.ap, xs_hi.ap), x3b + [xs_hi.buf], [xs_lo.buf],
                f"xlo_{tb}")

        def p1_proj(tb, names):
            ts = tb * TB1
            for nm in names:
                pj = {"q": 0, "k": 1, "v": 2}[nm]
                w_h, w_l = w_prj[nm]
                pw = (psum["pm0"], psum["pm1"], psum["pa0"])[pj]
                i_mm = 0
                for ch in range(NCH):
                    for (wv, xv) in ((w_h, xs_hi), (w_h, xs_lo), (w_l, xs_hi)):
                        mm(pw.ap, wv.ap[:, ch, :], xv.ap[:, ch, :],
                           [wv.buf, xv.buf], [pw.buf],
                           i_mm == 0, False, f"{nm}mm_{tb}_{ch}")
                        i_mm += 1
                mm(pw.ap, wsumn.ap[0:1, pj * H:(pj + 1) * H], rowA.ap,
                   [wsumn.buf, rowA.buf], [pw.buf], False, True, f"{nm}corr_{tb}")
                bias = bqkv.ap[:, pj:pj + 1]
                if nm == "v":
                    vop(t_mul(v_f.ap, pw.ap, bc0.ap), [pw.buf, bc0.buf],
                        [v_f.buf], f"vnrm_{tb}")
                    act(v_f.ap, v_f.ap, Act.Identity, [v_f.buf, bqkv.buf],
                        [v_f.buf], bias=bias, tag=f"vevac_{tb}")
                    for j in range(4):
                        i = 4 * tb + j
                        s = vstg[j % 2]
                        pv = psum["pa0"] if j % 2 == 0 else psum["pa1"]
                        tr.op("tensor",
                              (lambda o, i_, idn: lambda e: e.transpose(o, i_, idn))
                              (pv.ap[:, 0:H], v_f.ap[:, j * H:(j + 1) * H],
                               ident_s.ap),
                              reads=[v_f.buf, ident_s.buf], writes=[pv.buf],
                              tag=f"vtp_{i}")
                        act(s.ap, pv.ap[:, 0:H], Act.Copy,
                            [pv.buf], [s.buf], tag=f"vtev_{i}")
                        act(vt_hi.ap[:, i * H:(i + 1) * H], s.ap, Act.Copy,
                            [s.buf], [vt_hi.buf], tag=f"vthi_{i}")
                        vop(t_sub(vt_lo.ap[:, i * H:(i + 1) * H], s.ap,
                                  vt_hi.ap[:, i * H:(i + 1) * H]),
                            [s.buf, vt_hi.buf], [vt_lo.buf], f"vtlo_{i}")
                elif nm == "k":
                    vop(t_mul(kstage.ap, pw.ap, bc0.ap), [pw.buf, bc0.buf],
                        [kstage.buf], f"knrm_{tb}")
                    act(k_hi.ap[:, ts:ts + 512], kstage.ap, Act.Identity,
                        [kstage.buf, bqkv.buf], [k_hi.buf], bias=bias,
                        tag=f"{nm}hi_{tb}")
                    vop((lambda o, i, s_, x: lambda e: e.scalar_tensor_tensor(
                        o, i, s_, x, op0=Alu.add, op1=Alu.subtract))
                        (k_lo.ap[:, ts:ts + 512], kstage.ap, bias,
                         k_hi.ap[:, ts:ts + 512]),
                        [kstage.buf, bqkv.buf, k_hi.buf], [k_lo.buf],
                        f"{nm}lo_{tb}")
                else:
                    qh, ql = q_ri[tb % 2]
                    vop(t_mul(qstage.ap, pw.ap, bc0.ap), [pw.buf, bc0.buf],
                        [qstage.buf], f"qnrm_{tb}")
                    act(qh.ap, qstage.ap, Act.Identity, [qstage.buf, bqkv.buf],
                        [qh.buf], bias=bias, tag=f"{nm}hi_{tb}")
                    vop((lambda o, i, s_, x: lambda e: e.scalar_tensor_tensor(
                        o, i, s_, x, op0=Alu.add, op1=Alu.subtract))
                        (ql.ap, qstage.ap, bias, qh.ap),
                        [qstage.buf, bqkv.buf, qh.buf], [ql.buf], f"{nm}lo_{tb}")

        def emit_slab(i):
            nm, srcd, ch = slabs[i]
            st, sh, sl = fst[i % 2], fsh[0], fsl[0]
            hi_d, lo_d = scr[nm]
            rows = slice(ch * 128, (ch + 1) * 128)
            hb = fst_hb[i % 2]
            dma("gpsimd", st.ap, srcd[rows, :], [st.buf] + hb, [st.buf] + hb,
                f"fsl_{nm}{ch}_in")
            vop(t_copy(sh.ap, st.ap), [st.buf], [sh.buf], f"fslh_{nm}{ch}",
                eng="gpsimd")
            vop(t_sub(sl.ap, st.ap, sh.ap), [st.buf, sh.buf], [sl.buf], f"fsll_{nm}{ch}",
                eng="gpsimd")
            dma("gpsimd", hi_d[rows, :], sh.ap, [sh.buf], [scr_b[nm][0]],
                f"fsl_{nm}{ch}_outh")
            dma("gpsimd", lo_d[rows, :], sl.ap, [sl.buf], [scr_b[nm][1]],
                f"fsl_{nm}{ch}_outl")

        def pump_slabs(n):
            while slab_i[0] < len(slabs) and n > 0:
                emit_slab(slab_i[0])
                slab_i[0] += 1
                n -= 1

        def ln2_stats_flush(qb):
            pts = qb * QB
            stats_mm(x2tb.ap, x2tb_b, psum["pbc0"], psum["pbc1"],
                     f"ln2_{qb}")
            ln_stats_rows(psum["pbc0"], psum["pbc1"],
                          r2row.ap[0:1, pts:pts + 512], r2row.buf,
                          mr2row.ap[0:1, pts:pts + 512], mr2row.buf, f"ln2_{qb}",
                          (rowA2, rowB2, rowN2))
            for ch in range(NCH):
                dma("sync", x2d_r[:, ch, pts:pts + 512], x2tb.ap[:, ch, :],
                    [x2tb_b[ch]], [x2d_b2[qb][ch]], f"x2out_{qb}_{ch}")

        def attention(qb):
            qs0 = qb * QB
            nkj = 4 * qb + 4
            qh, ql = q_ri[qb % 2]
            first_sum = [True]
            first_av = [True]

            def st_chain(kj):
                nlo = max(0, 128 * kj - qs0)
                sl = kj % ERING
                pst = (psum["pm0"], psum["pm1"], psum["pa0"], psum["pa1"],
                       psum["pbc0"], psum["pbc1"])[kj % 6]
                ef = ef32[kj % 3]
                prs = [(k_hi.buf, k_hi, qh.buf, qh), (k_hi.buf, k_hi, ql.buf, ql),
                       (k_lo.buf, k_lo, qh.buf, qh)]
                for i_p, (kb, kv, qb_, qv) in enumerate(prs):
                    mm(pst.ap[:, nlo:512], kv.ap[:, kj * H:(kj + 1) * H],
                       qv.ap[:, nlo:512],
                       [kb, qb_], [pst.buf], i_p == 0, i_p == 2, f"st_{qb}_{kj}")
                act(ef.ap[:, nlo:512], pst.ap[:, nlo:512], Act.Exp,
                    [pst.buf], [ef.buf], scale=SCALE, tag=f"exp_{qb}_{kj}")
                act(e_hi.ap[:, sl, nlo:512], ef.ap[:, nlo:512], Act.Copy,
                    [ef.buf], [e_hi_b[sl]], tag=f"ehi_{qb}_{kj}")
                vop(t_sub(e_lo.ap[:, sl, nlo:512], ef.ap[:, nlo:512],
                          e_hi.ap[:, sl, nlo:512]),
                    [ef.buf, e_hi_b[sl]], [e_lo_b[sl]], f"elo_{qb}_{kj}")
                if 128 * kj >= qs0:
                    for ev, eb in ((e_hi, e_hi_b[sl]), (e_lo, e_lo_b[sl])):
                        vop(t_mul(ev.ap[:, sl, nlo:nlo + 128],
                                  ev.ap[:, sl, nlo:nlo + 128], tri_s.ap),
                            [eb, tri_s.buf], [eb], f"mask_{qb}_{kj}")

            def sums_av(kj, last_kj):
                nlo = max(0, 128 * kj - qs0)
                sl = kj % ERING
                mm(psum["prow0"].ap[:, nlo:512], ones_sq.ap,
                   e_hi.ap[:, sl, nlo:512], [ones_sq.buf, e_hi_b[sl]],
                   [psum["prow0"].buf], first_sum[0], False, f"sum_{qb}_{kj}")
                first_sum[0] = False
                for vv, vb in ((vt_hi, vt_hi.buf), (vt_lo, vt_lo.buf)):
                    mm(psum["prow1"].ap[:, nlo:512], vv.ap[:, kj * H:(kj + 1) * H],
                       e_hi.ap[:, sl, nlo:512], [vb, e_hi_b[sl]], [psum["prow1"].buf],
                       first_av[0], False, f"av_{qb}_{kj}")
                    first_av[0] = False
                mm(psum["prow0"].ap[:, nlo:512], ones_sq.ap,
                   e_lo.ap[:, sl, nlo:512], [ones_sq.buf, e_lo_b[sl]],
                   [psum["prow0"].buf], False, last_kj, f"sum_{qb}_{kj}l")
                mm(psum["prow1"].ap[:, nlo:512], vt_hi.ap[:, kj * H:(kj + 1) * H],
                   e_lo.ap[:, sl, nlo:512], [vt_hi.buf, e_lo_b[sl]],
                   [psum["prow1"].buf], False, last_kj, f"av_{qb}_{kj}l")

            ahead = min(5, nkj - 1)
            for kj in range(nkj):
                st_chain(kj)
                if kj >= ahead:
                    sums_av(kj - ahead, False)
            for kj in range(nkj - ahead, nkj):
                sums_av(kj, kj == nkj - 1)
            act(stmp.ap, psum["prow0"].ap[0:1, :], Act.Ln,
                [psum["prow0"].buf], [stmp.buf], tag=f"recln_{qb}")
            act(rs_row.ap, stmp.ap, Act.Exp, [stmp.buf], [rs_row.buf], scale=-1.0,
                tag=f"recip_{qb}")
            mm(psum["pa0"].ap, ones_k1f.ap, rs_row.ap, [ones_k1f.buf, rs_row.buf],
               [psum["pa0"].buf], True, True, f"rsbc_{qb}")
            vop(t_copy(rsbc.ap, psum["pa0"].ap), [psum["pa0"].buf], [rsbc.buf],
                f"rsev_{qb}")
            vop(t_mul(of_blk.ap, psum["prow1"].ap, rsbc.ap),
                [psum["prow1"].buf, rsbc.buf], [of_blk.buf], f"onrm_{qb}")
            act(o_hi.ap, of_blk.ap, Act.Copy, [of_blk.buf], [o_hi.buf],
                tag=f"ohi_{qb}")
            vop(t_sub(o_lo.ap, of_blk.ap, o_hi.ap),
                [of_blk.buf, o_hi.buf], [o_lo.buf], f"olo_{qb}")

        def wo_block(qb):
            ts = qb * QB
            for dch in range(NCH):
                pw = (psum["pa0"], psum["pa1"], psum["pm0"],
                      psum["pm1"])[dch % 4]
                prs = [(wo_hi, o_hi), (wo_lo, o_hi), (wo_hi, o_lo)]
                for i_p, (wv, ov) in enumerate(prs):
                    mm(pw.ap, wv.ap[:, dch * H:(dch + 1) * H], ov.ap,
                       [wv.buf, ov.buf], [pw.buf], i_p == 0, i_p == 2, f"wo_{qb}_{dch}")
                vop((lambda o, i, s, x: lambda e: e.scalar_tensor_tensor(
                    o, i, s, x, op0=Alu.add, op1=Alu.add))
                    (x2tb.ap[:, dch, :], pw.ap, bo_s.ap[:, dch:dch + 1],
                     x_ring[qb % 3].ap[:, dch, :]),
                    [pw.buf, bo_s.buf, x_ring_b[qb % 3][dch]], [x2tb_b[dch]],
                    f"woev_{qb}_{dch}")

        xsplit(0)
        stats_mm(x_ring[0].ap, x_ring_b[0], psum["prow0"], psum["prow1"],
                 "ln1_0")
        p1_rows(0)
        emit_wsplits()
        emit_wsums()
        p1_bcast(0)
        for tb in range(NTB1):
            p1_proj(tb, ("q",))
            p1_proj(tb, ("k", "v"))
            attention(tb)
            if tb + 1 < NTB1:
                xsplit(tb + 1)
                stats_mm(x_ring[(tb + 1) % 3].ap, x_ring_b[(tb + 1) % 3],
                         psum["prow0"], psum["prow1"], f"ln1_{tb + 1}")
                p1_rows(tb + 1)
            wo_block(tb)
            if tb < NTB1 - 1:
                ln2_stats_flush(tb)
            if tb + 1 < NTB1:
                p1_bcast(tb + 1)
            if tb + 3 < NTB1:
                ts3 = (tb + 3) * TB1
                for ch in range(NCH):
                    dma("sync", x_ring[(tb + 3) % 3].ap[:, ch, :],
                        xT_r[:, ch, ts3:ts3 + 512],
                        [x_ring_b[(tb + 3) % 3][ch]], [x_ring_b[(tb + 3) % 3][ch]],
                        f"x_in{tb + 3}_{ch}")
            pump_slabs(6)
        pump_slabs(len(slabs))

        if debug:
            dma("sync", dbg["d_khi"], k_hi.ap, [k_hi.buf], [], "dbg")
            dma("sync", dbg["d_klo"], k_lo.ap, [k_lo.buf], [], "dbg")
            dma("sync", dbg["d_r2"], r2row.ap, [r2row.buf], [], "dbg")
            dma("sync", dbg["d_mr2"], mr2row.ap, [mr2row.buf], [], "dbg")

        A.free(xs_hi); A.free(xs_lo)
        A.free(bc0); A.free(qstage); A.free(kstage); A.free(wsumn)
        for v in fst + fsh + fsl:
            A.free(v)
        A.free(e_hi); A.free(e_lo)
        for v in ef32:
            A.free(v)
        A.free(of_blk); A.free(rsbc); A.free(rs_row)
        tr.barrier()

        for v in x_ring:
            A.free(v)
        A.free(o_hi); A.free(o_lo); A.free(wo_hi); A.free(wo_lo)
        for qv in q_ri:
            A.free(qv[0]); A.free(qv[1])
        for v in (k_hi, k_lo, v_f, vt_hi, vt_lo):
            A.free(v)
        A.free(rowA); A.free(rowB); A.free(rowR); A.free(rowMR)
        A.free(rowN1)
        A.free(stmp)
        A.free(vstg[0]); A.free(vstg[1])
        A.free(ident_s); A.free(tri_s); A.free(ones_k1); A.free(ones_m1)
        A.free(ones_sq)
        x2l = [A.alloc(f"x2l{i}", (128, NCH, TBF), F32) for i in range(2)]
        x2l_b = [[Buf(f"x2l{i}c{c}") for c in range(NCH)] for i in range(2)]
        h2b = A.alloc("h2b", (128, NCH, TBF + 1), F32)
        h2b_b = [Buf(f"h2b{c}") for c in range(NCH)]

        xfa = [A.alloc(f"xfa{i}", (128, TBF), F32) for i in range(4)]
        krl = [xfa[0], xfa[1]]
        xr_hi = [A.alloc("xr_hi0", (128, NCH, TBF), BF16)]
        xr_lo = [A.alloc("xr_lo0", (128, NCH, TBF), BF16)]
        xk_hi = [A.alloc("xk_hi0", (128, NCH, TBF), BF16)]
        xk_lo = [A.alloc("xk_lo0", (128, NCH, TBF), BF16)]
        kc_hi = A.alloc("kc_hi", (128, NCH, TBF), BF16)
        kc_lo = A.alloc("kc_lo", (128, NCH, TBF), BF16)
        xr_hi_b = [Buf(f"xrh{c}") for c in range(NCH)]
        xr_lo_b = [Buf(f"xrl{c}") for c in range(NCH)]
        xk_hi_b = [Buf(f"xkh{c}") for c in range(NCH)]
        xk_lo_b = [Buf(f"xkl{c}") for c in range(NCH)]
        kc_hi_b = [Buf(f"kch{c}") for c in range(NCH)]
        kc_lo_b = [Buf(f"kcl{c}") for c in range(NCH)]
        dbf_b = [Buf(f"dbf{c}") for c in range(NCH)]

        bcr0 = A.alloc("bcr0", (128, TBF), F32)
        bcr1 = A.alloc("bcr1", (128, TBF), F32)
        vtmp = [A.alloc(f"vtm{i}", (128, TBF), F32) for i in range(2)]
        wbh = [A.alloc(f"wbh{i}", (128, NCH, H), BF16) for i in range(4)]
        wbl = [A.alloc(f"wbl{i}", (128, NCH, H), BF16) for i in range(4)]
        wsched = [(k_, t_, d_) for t_ in range(NTBF) for k_ in ("r", "k", "v")
                  for d_ in range(NCH)]

        def emit_wst(g):
            if g >= len(wsched):
                return
            stg, t_, d_ = wsched[g]
            hi_d, lo_d = scr[stg]
            wh, wl = wbh[g % 4], wbl[g % 4]
            cols = slice(d_ * H, (d_ + 1) * H)
            dma("sync", wh.ap, hi_d[:, cols].rearrange("(c p) d -> p c d", p=128),
                [wh.buf, scr_b[stg][0]], [wh.buf], f"wst_{stg}_{t_}_{d_}h")
            dma("sync", wl.ap, lo_d[:, cols].rearrange("(c p) d -> p c d", p=128),
                [wl.buf, scr_b[stg][1]], [wl.buf], f"wst_{stg}_{t_}_{d_}l")
        pbank6 = (psum["pm0"], psum["pm1"], psum["pa0"], psum["pa1"],
                  psum["prow0"], psum["prow1"])

        def ffn_stage(tbf, ts, stage, xin_hi, xin_hi_b, xin_lo, xin_lo_b,
                      kind, xcur, mix_pump=None):
            xcur_b = x2l_b[tbf % 2]
            base = {"r": 0, "k": 2, "v": 4}[kind]
            g0 = tbf * 24 + {"r": 0, "k": 8, "v": 16}[kind]
            for dch in range(NCH):
                g = g0 + dch
                wh, wl = wbh[g % 4], wbl[g % 4]
                emit_wst(g + 2)
                if mix_pump:
                    for _ in range(2):
                        if mix_pump:
                            emit_mix_item(mix_pump.pop(0))
                pw = pbank6[(base + dch) % 6]
                i_mm = 0
                for ch in range(NCH):
                    for (wv, xv, xb) in ((wh, xin_hi, xin_hi_b[ch]),
                                         (wh, xin_lo, xin_lo_b[ch]),
                                         (wl, xin_hi, xin_hi_b[ch])):
                        mm(pw.ap[:, 0:TBF], wv.ap[:, ch, :], xv.ap[:, ch, :],
                           [wv.buf, xb], [pw.buf], i_mm == 0, i_mm == NCH * 3 - 1,
                           f"f{stage}_{tbf}_{dch}_{ch}")
                        i_mm += 1
                if kind == "r":
                    act(rgate.ap[:, dch, :], pw.ap[:, 0:TBF], Act.Sigmoid,
                        [pw.buf], [rgate.buf], tag=f"sig_{tbf}_{dch}")
                elif kind == "k":
                    kr = krl[dch % 2]
                    act(kr.ap, pw.ap[:, 0:TBF], Act.Relu, [pw.buf], [kr.buf],
                        tag=f"relu_{tbf}_{dch}")
                    vop((lambda o, a: lambda e: e.tensor_mul(o, a, a))
                        (dbf.ap[:, dch, :], kr.ap), [kr.buf], [dbf_b[dch]],
                        f"ksq_{tbf}_{dch}")
                    act(kc_hi.ap[:, dch, :], dbf.ap[:, dch, :], Act.Copy,
                        [dbf_b[dch]], [kc_hi_b[dch]], tag=f"kchi_{tbf}_{dch}")
                    vop(t_sub(kc_lo.ap[:, dch, :], dbf.ap[:, dch, :],
                              kc_hi.ap[:, dch, :]),
                        [dbf_b[dch], kc_hi_b[dch]], [kc_lo_b[dch]],
                        f"kclo_{tbf}_{dch}")
                else:
                    vt = vtmp[dch % 2]
                    vop(t_mul(vt.ap, pw.ap[:, 0:TBF], rgate.ap[:, dch, :]),
                        [pw.buf, rgate.buf], [vt.buf], f"vmul_{tbf}_{dch}")
                    vop(t_add(xcur.ap[:, dch, :], xcur.ap[:, dch, :], vt.ap),
                        [xcur_b[dch], vt.buf], [xcur_b[dch]], f"yadd_{tbf}_{dch}",
                        eng="gpsimd")

        def p5_mix_head(tbf):
            ts = tbf * TBF
            if tbf > 0:
                vop(t_copy(h2b.ap[:, :, 0:1], h2b.ap[:, :, TBF:TBF + 1]),
                    h2b_b, h2b_b, f"h2bd_{tbf}")
            bcast_pair(r2row.ap[0:1, ts:ts + TBF], r2row.buf,
                       mr2row.ap[0:1, ts:ts + TBF], mr2row.buf, bcr0, bcr1, TBF,
                       f"bc2_{tbf}")

        def emit_mix_item(it, flip=False):
            ch, tbf = it
            ts = tbf * TBF
            xc = x2l[tbf % 2]
            xcb = x2l_b[tbf % 2][ch]
            hv = "gpsimd" if flip else "vector"
            lo_e = "vector" if flip else "gpsimd"
            dma("sync", xc.ap[:, ch, :], x2d_r[:, ch, ts:ts + TBF],
                [x2d_b2[tbf][ch], xcb], [xcb], f"x2in_{tbf}_{ch}")
            vop(t_mul(h2b.ap[:, ch, 1:TBF + 1], xc.ap[:, ch, :],
                      bcr0.ap[:, 0:TBF]),
                [xcb, bcr0.buf, h2b_b[ch]], [h2b_b[ch]], f"h2m_{tbf}_{ch}",
                eng=hv)
            vop(t_sub(h2b.ap[:, ch, 1:TBF + 1], h2b.ap[:, ch, 1:TBF + 1],
                      bcr1.ap[:, 0:TBF]),
                [h2b_b[ch], bcr1.buf], [h2b_b[ch]], f"h2s_{tbf}_{ch}", eng=hv)
            vop(t_sub(dbf.ap[:, ch, :], h2b.ap[:, ch, 1:TBF + 1],
                      h2b.ap[:, ch, 0:TBF]),
                [h2b_b[ch]], [dbf_b[ch]], f"dmix_{tbf}_{ch}", eng=hv)
            for mu_s, dsth, dsth_b, dstl, dstl_b, nm in (
                    (mur_s, xr_hi[0], xr_hi_b, xr_lo[0], xr_lo_b, "r"),
                    (muk_s, xk_hi[0], xk_hi_b, xk_lo[0], xk_lo_b, "k")):
                xf = xfa[(2 * ch + (0 if nm == "r" else 1)) % 4]
                vop((lambda o, i, s, x: lambda e: e.scalar_tensor_tensor(
                    o, i, s, x, op0=Alu.mult, op1=Alu.add))
                    (xf.ap, dbf.ap[:, ch, :], mu_s.ap[:, ch:ch + 1],
                     h2b.ap[:, ch, 0:TBF]),
                    [dbf_b[ch], mu_s.buf, h2b_b[ch]], [xf.buf], f"x{nm}_{tbf}_{ch}")
                act(dsth.ap[:, ch, :], xf.ap, Act.Copy, [xf.buf], [dsth_b[ch]],
                    tag=f"x{nm}h_{tbf}_{ch}")
                vop(t_sub(dstl.ap[:, ch, :], xf.ap, dsth.ap[:, ch, :]),
                    [xf.buf, dsth_b[ch]], [dstl_b[ch]], f"x{nm}l_{tbf}_{ch}",
                    eng=lo_e)

        def p5_mix_items(tbf):
            return [(ch, tbf) for ch in range(NCH)]

        vop((lambda a: lambda e: e.memset(a, 0.0))(h2b.ap[:, :, 0:1]),
            [], h2b_b, "h2z")
        emit_wst(0)
        emit_wst(1)
        ln2_stats_flush(NTB1 - 1)
        A.free(x2tb); A.free(sq[0]); A.free(sq[1])
        A.free(rowA2); A.free(rowB2); A.free(rowN2)
        dbf = A.alloc("dbf", (128, NCH, TBF), F32)
        rgate = A.alloc("rgate", (128, NCH, TBF), F32)
        p5_mix_head(0)
        for it in p5_mix_items(0):
            emit_mix_item(it)
        for tbf in range(NTBF):
            ts = tbf * TBF
            xc = x2l[tbf % 2]
            ffn_stage(tbf, ts, "r", xr_hi[0], xr_hi_b, xr_lo[0], xr_lo_b, "r", xc)
            ffn_stage(tbf, ts, "k", xk_hi[0], xk_hi_b, xk_lo[0], xk_lo_b, "k", xc)
            pump = None
            if tbf + 1 < NTBF:
                p5_mix_head(tbf + 1)
                pump = p5_mix_items(tbf + 1)
            ffn_stage(tbf, ts, "v", kc_hi, kc_hi_b, kc_lo, kc_lo_b, "v", xc,
                      mix_pump=pump)
            if debug and tbf == 0:
                dma("sync", dbg["d_rg"], rgate.ap.rearrange("p a b -> p (a b)"),
                    [rgate.buf], [], "dbg")
                dma("sync", dbg["d_kch"], kc_hi.ap.rearrange("p a b -> p (a b)"),
                    kc_hi_b, [], "dbg")
                dma("sync", dbg["d_kcl"], kc_lo.ap.rearrange("p a b -> p (a b)"),
                    kc_lo_b, [], "dbg")
                dma("sync", dbg["d_xkh"], xk_hi[0].ap.rearrange("p a b -> p (a b)"),
                    xk_hi_b, [], "dbg")
                dma("sync", dbg["d_xkl"], xk_lo[0].ap.rearrange("p a b -> p (a b)"),
                    xk_lo_b, [], "dbg")
            dma("gpsimd", yT_r[:, :, ts:ts + TBF], xc.ap, x2l_b[tbf % 2], [],
                f"yout_{tbf}")

        tr.barrier()
        tr.resolve()
        tr.emit(block, sems)
    return nc


_NC = {}


def _get_nc(debug=False):
    if debug not in _NC:
        _NC[debug] = build_program(debug=debug)
    return _NC[debug]


def make_in_maps(L0_attn_x, L0_attn_Wq, L0_attn_bq, L0_attn_Wk, L0_attn_bk,
                 L0_attn_Wv, L0_attn_bv, L0_attn_Wo, L0_attn_bo,
                 L0_ffn_W_r, L0_ffn_W_k, L0_ffn_W_v, L0_ffn_mu_r, L0_ffn_mu_k):
    x = np.asarray(L0_attn_x, dtype=np.float32)
    common = {
        "wqT": np.ascontiguousarray(np.asarray(L0_attn_Wq, dtype=np.float32).T),
        "wkT": np.ascontiguousarray(np.asarray(L0_attn_Wk, dtype=np.float32).T),
        "wvT": np.ascontiguousarray(np.asarray(L0_attn_Wv, dtype=np.float32).T),
        "bqkv": np.ascontiguousarray(np.stack(
            [np.asarray(L0_attn_bq, dtype=np.float32),
             np.asarray(L0_attn_bk, dtype=np.float32),
             np.asarray(L0_attn_bv, dtype=np.float32)], axis=1)),
        "woT": np.ascontiguousarray(np.asarray(L0_attn_Wo, dtype=np.float32).T),
        "bo": np.asarray(L0_attn_bo, dtype=np.float32),
        "frT": np.ascontiguousarray(np.asarray(L0_ffn_W_r, dtype=np.float32).T),
        "fkT": np.ascontiguousarray(np.asarray(L0_ffn_W_k, dtype=np.float32).T),
        "fvT": np.ascontiguousarray(np.asarray(L0_ffn_W_v, dtype=np.float32).T),
        "mur": np.asarray(L0_ffn_mu_r, dtype=np.float32),
        "muk": np.asarray(L0_ffn_mu_k, dtype=np.float32),
        "tri": np.triu(np.ones((H, H))).astype(ml_dtypes.bfloat16),
        "ident": np.eye(H, dtype=np.float32),
    }
    in_maps = []
    for b in range(8):
        m = dict(common)
        m["xT"] = np.ascontiguousarray(x[b].T)
        in_maps.append(m)
    return in_maps


def kernel(**inputs):
    nc = _get_nc(debug=False)
    in_maps = make_in_maps(**inputs)
    res = run_bass_kernel_spmd(nc, in_maps, core_ids=list(range(8)))
    y = np.stack([np.asarray(res.results[b]["yT"]).T for b in range(8)], axis=0)
    return np.ascontiguousarray(y, dtype=np.float32)


# version 44
# speedup vs baseline: 1.0444x; 1.0444x over previous
import contextlib
import numpy as np
import ml_dtypes
import concourse.bass as bass
import concourse.mybir as mybir
from concourse.bass_utils import run_bass_kernel_spmd

F32 = mybir.dt.float32
BF16 = mybir.dt.bfloat16
Alu = mybir.AluOpType
Act = mybir.ActivationFunctionType
Ax = mybir.AxisListType

C, T, H, NCH = 1024, 2048, 128, 8
TB1, NTB1 = 512, 4
QB, NQB = 512, 4
TBF, NTBF = 512, 4
KJ = 16
SCALE = float(np.float32(128.0) ** np.float32(-0.5))
EPS = 1e-5
ARENA_W = 52736

ENGS = ("sync", "scalar", "vector", "tensor", "gpsimd")
NQSLOT = 10


class Buf:
    __slots__ = ("name", "writer", "readers")

    def __init__(self, name):
        self.name = name
        self.writer = None
        self.readers = {}


class Op:
    __slots__ = ("eng", "seq", "fn", "is_dma", "deps", "needs_inc", "inc_val",
                 "tag", "qslot")


class Tracker:

    def __init__(self):
        self.ops = {e: [] for e in ENGS}
        self.last_real = {e: None for e in ENGS}
        self.dma_cnt = {e: 0 for e in ENGS}
        self.slot_last = {}

    def op(self, eng, fn, reads=(), writes=(), is_dma=False, tag=""):
        o = Op()
        o.eng, o.fn, o.is_dma, o.tag = eng, fn, is_dma, tag
        o.seq = len(self.ops[eng])
        o.needs_inc = False
        o.inc_val = None
        o.qslot = None
        if is_dma:
            idx = self.dma_cnt[eng]
            self.dma_cnt[eng] = idx + 1
            o.qslot = idx % NQSLOT
            o.inc_val = (idx // NQSLOT + 1) * 16
            o.needs_inc = True
        dep = set()
        for b in reads:
            w = b.writer
            if w is not None and (w.is_dma or w.eng != eng):
                dep.add(w)
            b.readers[(eng, is_dma, o.qslot)] = o
        for b in writes:
            w = b.writer
            if w is not None and (w.is_dma or w.eng != eng):
                dep.add(w)
            for r in b.readers.values():
                if r is not o and (r.is_dma or r.eng != eng):
                    dep.add(r)
            b.writer = o
            b.readers = {}
        if is_dma:
            prev = self.slot_last.get((eng, o.qslot))
            if prev is not None:
                dep.add(prev)
            self.slot_last[(eng, o.qslot)] = o
        o.deps = dep
        self.ops[eng].append(o)
        if fn is not None:
            self.last_real[eng] = o
        return o

    def barrier(self):
        lasts = [self.last_real[e] for e in ENGS if self.last_real[e] is not None]
        lasts += list(self.slot_last.values())
        for e in ENGS:
            o = Op()
            o.eng, o.fn, o.is_dma, o.tag = e, None, False, "barrier"
            o.seq = len(self.ops[e])
            o.needs_inc = False
            o.inc_val = None
            o.qslot = None
            o.deps = {l for l in lasts if l.eng != e or l.is_dma}
            self.ops[e].append(o)

    def resolve(self):
        for e in ENGS:
            for o in self.ops[e]:
                for d in o.deps:
                    d.needs_inc = True
        for e in ENGS:
            cnt = 0
            for o in self.ops[e]:
                if o.is_dma:
                    continue
                if o.needs_inc:
                    if o.fn is None:
                        raise RuntimeError(f"dep on pure-wait op {o.eng}#{o.seq}")
                    cnt += 1
                    o.inc_val = cnt
        return self

    def emit(self, block, sems):
        def mk(section_name):
            ops = self.ops[section_name]

            def body(engine):
                covered = {}
                for o in ops:
                    for d in sorted(o.deps,
                                    key=lambda d: (d.eng, d.is_dma, d.qslot or 0,
                                                   d.inc_val)):
                        key = (("q", d.eng, d.qslot) if d.is_dma else ("c", d.eng))
                        v = d.inc_val
                        assert v is not None, (d.eng, d.seq, d.tag)
                        if covered.get(key, 0) < v:
                            engine.wait_ge(sems[key], v)
                            covered[key] = v
                    if o.fn is None:
                        continue
                    inst = o.fn(engine)
                    if o.needs_inc:
                        key = (("q", o.eng, o.qslot) if o.is_dma else ("c", o.eng))
                        inst.then_inc(sems[key], 16 if o.is_dma else 1)
            return body

        for e in ENGS:
            getattr(block, e)(mk(e))


class View:
    __slots__ = ("ap", "buf", "off", "words")

    def __getitem__(self, idx):
        return self.ap[idx]


class Arena:
    def __init__(self, tensor, words):
        self.t = tensor
        self.free_list = [(0, words)]

    def alloc(self, name, shape, dtype):
        per_part = 1
        for s in shape[1:]:
            per_part *= s
        w = per_part if dtype == F32 else (per_part + 1) // 2
        w = (w + 1) // 2 * 2
        for i, (off, sz) in enumerate(self.free_list):
            if sz >= w:
                if sz == w:
                    self.free_list.pop(i)
                else:
                    self.free_list[i] = (off + w, sz - w)
                v = View()
                v.off, v.words = off, w
                ap = self.t[0:shape[0], off:off + w]
                if dtype == BF16:
                    ap = ap.bitcast(BF16)
                n_free = per_part
                if len(shape) == 2:
                    ap = ap[:, 0:shape[1]]
                elif len(shape) == 3:
                    ap = ap[:, 0:n_free].rearrange("p (a b) -> p a b", a=shape[1])
                else:
                    raise ValueError(shape)
                v.ap = ap
                v.buf = Buf(name)
                return v
        raise MemoryError(f"arena OOM for {name} ({w} words); free={self.free_list}")

    def free(self, v):
        fl = self.free_list
        fl.append((v.off, v.words))
        fl.sort()
        out = []
        for off, sz in fl:
            if out and out[-1][0] + out[-1][1] == off:
                out[-1] = (out[-1][0], out[-1][1] + sz)
            else:
                out.append((off, sz))
        self.free_list = out


def build_program(debug=False):
    nc = bass.Bass("TRN2", target_bir_lowering=False)

    xT = nc.dram_tensor("xT", [C, T], F32, kind="ExternalInput").ap()
    wqT = nc.dram_tensor("wqT", [C, H], F32, kind="ExternalInput").ap()
    wkT = nc.dram_tensor("wkT", [C, H], F32, kind="ExternalInput").ap()
    wvT = nc.dram_tensor("wvT", [C, H], F32, kind="ExternalInput").ap()
    bqkv_d = nc.dram_tensor("bqkv", [H, 3], F32, kind="ExternalInput").ap()
    woT = nc.dram_tensor("woT", [H, C], F32, kind="ExternalInput").ap()
    bo_d = nc.dram_tensor("bo", [C], F32, kind="ExternalInput").ap()
    frT = nc.dram_tensor("frT", [C, C], F32, kind="ExternalInput").ap()
    fkT = nc.dram_tensor("fkT", [C, C], F32, kind="ExternalInput").ap()
    fvT = nc.dram_tensor("fvT", [C, C], F32, kind="ExternalInput").ap()
    mur_d = nc.dram_tensor("mur", [C], F32, kind="ExternalInput").ap()
    muk_d = nc.dram_tensor("muk", [C], F32, kind="ExternalInput").ap()
    tri_d = nc.dram_tensor("tri", [H, H], BF16, kind="ExternalInput").ap()
    ident_d = nc.dram_tensor("ident", [H, H], F32, kind="ExternalInput").ap()
    yT = nc.dram_tensor("yT", [C, T], F32, kind="ExternalOutput").ap()

    scr = {}
    scr_b = {}
    for nm in ("r", "k", "v"):
        scr[nm] = (nc.dram_tensor(f"s_hi_{nm}", [C, C], BF16).ap(),
                   nc.dram_tensor(f"s_lo_{nm}", [C, C], BF16).ap())
        scr_b[nm] = (Buf(f"s_hi_{nm}"), Buf(f"s_lo_{nm}"))
    x2d = nc.dram_tensor("x2d", [C, T], F32).ap()
    x2d_b2 = [[Buf(f"x2d{q}c{c}") for c in range(NCH)] for q in range(4)]
    x2d_r = x2d.rearrange("(c p) t -> p c t", p=128)

    dbg = {}
    if debug:
        for nm, shp in (("d_qhi", [H, T]), ("d_qlo", [H, T]), ("d_khi", [H, T]),
                        ("d_klo", [H, T]), ("d_vf", [H, T]), ("d_mil", [1, QB]),
                        ("d_ohi", [H, T]), ("d_olo", [H, T]), ("d_x2", [C, T]),
                        ("d_r2", [1, T]), ("d_mr2", [1, T]), ("d_rg", [C, TBF]),
                        ("d_ehi", [H, KJ * QB]), ("d_elo", [H, KJ * QB]),
                        ("d_kch", [H, NCH * TBF]), ("d_kcl", [H, NCH * TBF]),
                        ("d_xkh", [H, NCH * TBF]), ("d_xkl", [H, NCH * TBF])):
            d = BF16 if nm in ("d_qhi", "d_qlo", "d_khi", "d_klo", "d_ohi",
                               "d_olo", "d_ehi", "d_elo", "d_kch", "d_kcl",
                               "d_xkh", "d_xkl") else F32
            dbg[nm] = nc.dram_tensor(nm, shp, d, kind="ExternalOutput").ap()

    tr = Tracker()
    xT_r = xT.rearrange("(c p) t -> p c t", p=128)
    yT_r = yT.rearrange("(c p) t -> p c t", p=128)

    with (
        nc.sbuf_tensor("arena", [128, ARENA_W], F32) as arena_t,
        nc.psum_tensor("pm0", [128, 512], F32) as pm0_t,
        nc.psum_tensor("pm1", [128, 512], F32) as pm1_t,
        nc.psum_tensor("pa0", [128, 512], F32) as pa0_t,
        nc.psum_tensor("pa1", [128, 512], F32) as pa1_t,
        nc.psum_tensor("prow0", [128, 512], F32) as prow0_t,
        nc.psum_tensor("prow1", [128, 512], F32) as prow1_t,
        nc.psum_tensor("pbc0", [128, 512], F32) as pbc0_t,
        nc.psum_tensor("pbc1", [128, 512], F32) as pbc1_t,
        contextlib.ExitStack() as _es,
        nc.Block() as block,
    ):
        sems = {}
        for e in ("sync", "scalar", "gpsimd"):
            for s in range(NQSLOT):
                sems[("q", e, s)] = _es.enter_context(nc.semaphore(f"q_{e}_{s}"))
        for e in ("scalar", "vector", "tensor", "gpsimd"):
            sems[("c", e)] = _es.enter_context(nc.semaphore(f"c_{e}"))

        A = Arena(arena_t, ARENA_W)
        psum = {}
        for nm, t in (("pm0", pm0_t), ("pm1", pm1_t), ("pa0", pa0_t), ("pa1", pa1_t),
                      ("prow0", prow0_t), ("prow1", prow1_t), ("pbc0", pbc0_t),
                      ("pbc1", pbc1_t)):
            v = View()
            v.ap, v.buf, v.off, v.words = t[:], Buf(nm), -1, 0
            psum[nm] = v

        x_ring = [A.alloc(f"x_ring{i}", (128, NCH, 512), F32) for i in range(3)]
        x_ring_b = [[Buf(f"x{i}c{c}") for c in range(NCH)] for i in range(3)]
        w_prj = {}
        for nm in ("q", "k", "v"):
            w_prj[nm] = (A.alloc(f"w{nm}_hi", (128, NCH, H), BF16),
                         A.alloc(f"w{nm}_lo", (128, NCH, H), BF16))
        wo_hi = A.alloc("wo_hi", (128, C), BF16)
        wo_lo = A.alloc("wo_lo", (128, C), BF16)
        bqkv = A.alloc("bqkv", (128, 4), F32)
        bo_s = A.alloc("bo_s", (128, NCH), F32)
        mur_s = A.alloc("mur_s", (128, NCH), F32)
        muk_s = A.alloc("muk_s", (128, NCH), F32)
        q_ri = [(A.alloc(f"qh{i}", (128, 512), BF16),
                 A.alloc(f"ql{i}", (128, 512), BF16)) for i in range(2)]
        k_hi = A.alloc("k_hi", (128, T), BF16)
        k_lo = A.alloc("k_lo", (128, T), BF16)
        v_f = A.alloc("v_f", (128, 512), F32)
        vt_hi = A.alloc("vt_hi", (128, T), BF16)
        vt_lo = A.alloc("vt_lo", (128, T), BF16)
        o_hi = A.alloc("o_hi", (128, 512), BF16)
        o_lo = A.alloc("o_lo", (128, 512), BF16)
        r2row = A.alloc("r2row", (1, T), F32)
        mr2row = A.alloc("mr2row", (1, T), F32)
        ERING = 6
        e_hi = A.alloc("e_hi", (128, ERING, QB), BF16)
        e_lo = A.alloc("e_lo", (128, ERING, QB), BF16)
        e_hi_b = [Buf(f"e_hi{j}") for j in range(ERING)]
        e_lo_b = [Buf(f"e_lo{j}") for j in range(ERING)]
        ef32 = [A.alloc(f"ef32{i}", (128, 512), F32) for i in range(3)]
        of_blk = A.alloc("of_blk", (128, 512), F32)
        rsbc = A.alloc("rsbc", (128, 512), F32)
        fst = [A.alloc(f"fst{i}", (128, C), F32) for i in range(2)]
        fsh = [A.alloc("fsh0", (128, C), BF16)]
        fsl = [A.alloc("fsl0", (128, C), BF16)]
        ident_s = A.alloc("ident_s", (128, H), F32)
        tri_s = A.alloc("tri_s", (128, H), BF16)
        ones_k1 = A.alloc("ones_k1", (1, 128), BF16)
        ones_k1f = A.alloc("ones_k1f", (1, 128), F32)
        ones_m1 = A.alloc("ones_m1", (128, 2), BF16)
        ones_sq = A.alloc("ones_sq", (128, H), BF16)
        ones_m1f = A.alloc("ones_m1f", (128, 2), F32)
        rowA = A.alloc("rowA", (1, 512), F32)
        rowB = A.alloc("rowB", (1, 512), F32)
        rowR = A.alloc("rowR", (1, 512), F32)
        rowMR = A.alloc("rowMR", (1, 512), F32)
        stmp = A.alloc("stmp", (1, 512), F32)
        rowA2 = A.alloc("rowA2", (1, 512), F32)
        rowB2 = A.alloc("rowB2", (1, 512), F32)
        rowN2 = A.alloc("rowN2", (1, 512), F32)
        rowN1 = A.alloc("rowN1", (1, 512), F32)
        rs_row = A.alloc("rs_row", (1, 512), F32)

        x2tb = A.alloc("x2tb", (128, NCH, 512), F32)
        x2tb_b = [Buf(f"x2tb{c}") for c in range(NCH)]
        xs_hi = A.alloc("xs_hi", (128, NCH, 512), BF16)
        xs_lo = A.alloc("xs_lo", (128, NCH, 512), BF16)
        bc0 = A.alloc("bc0", (128, 512), F32)
        qstage = A.alloc("qstage", (128, 512), F32)
        kstage = A.alloc("kstage", (128, 512), F32)
        sq = [A.alloc(f"sq{i}", (128, 512), F32) for i in range(2)]
        vstg = [A.alloc(f"vstg{i}", (128, H), F32) for i in range(2)]
        wsumn = A.alloc("wsumn", (1, 3 * H), F32)

        def dma(eng, out_ap, in_ap, reads, writes, tag="", slow=False):
            tr.op(eng, (lambda o, i, s: lambda e: e.dma_start(
                o, i, allow_slow_non_contiguous=s))(out_ap, in_ap, slow),
                  reads=reads, writes=writes, is_dma=True, tag=tag)

        def mm(out_ap, l_ap, r_ap, reads, writes, start, stop, tag=""):
            tr.op("tensor",
                  (lambda o, l, r, st, sp: lambda e: e.matmul(o, l, r, start=st, stop=sp))
                  (out_ap, l_ap, r_ap, start, stop),
                  reads=reads, writes=writes, tag=tag)

        def act(out_ap, in_ap, func, reads, writes, bias=None, scale=None, tag=""):
            def f(o, i, fu, b, s):
                def run(e):
                    kw = {}
                    if b is not None:
                        kw["bias"] = b
                    if s is not None:
                        kw["scale"] = s
                    return e.activation(o, i, fu, **kw)
                return run
            tr.op("scalar", f(out_ap, in_ap, func, bias, scale),
                  reads=reads, writes=writes, tag=tag)

        def vop(fn, reads, writes, tag="", eng="vector"):
            tr.op(eng, fn, reads=reads, writes=writes, tag=tag)

        def t_sub(o, a, b):
            return lambda e: e.tensor_sub(o, a, b)

        def t_mul(o, a, b):
            return lambda e: e.tensor_mul(o, a, b)

        def t_add(o, a, b):
            return lambda e: e.tensor_add(o, a, b)

        def t_copy(o, a):
            return lambda e: e.tensor_copy(o, a)

        for tb in range(3):
            ts = tb * TB1
            for ch in range(NCH):
                dma("sync", x_ring[tb].ap[:, ch, :], xT_r[:, ch, ts:ts + 512],
                    [x_ring_b[tb][ch]], [x_ring_b[tb][ch]], f"x_in{tb}_{ch}")
        dma("sync", tri_s.ap, tri_d, [], [tri_s.buf])
        dma("sync", ident_s.ap, ident_d, [], [ident_s.buf])
        dma("sync", bqkv.ap[:, 0:3], bqkv_d, [], [bqkv.buf])
        dma("sync", bo_s.ap, bo_d.rearrange("(c p) -> p c", p=128), [], [bo_s.buf],
            slow=True)
        dma("sync", mur_s.ap, mur_d.rearrange("(c p) -> p c", p=128), [], [mur_s.buf],
            slow=True)
        dma("sync", muk_s.ap, muk_d.rearrange("(c p) -> p c", p=128), [], [muk_s.buf],
            slow=True)
        for v, val in ((ones_k1, 1.0), (ones_k1f, 1.0), (ones_m1, 1.0),
                       (ones_m1f, 1.0), (ones_sq, 1.0)):
            vop((lambda a, c_: lambda e: e.memset(a, c_))(v.ap, val), [], [v.buf],
                "memset")

        fst_hb = [[Buf(f"fst{i}h{h}") for h in range(2)] for i in range(2)]
        w_half = {"q": [(fst[0].ap[:, 0:512], fst_hb[0][0]),
                        (fst[0].ap[:, 512:1024], fst_hb[0][1])],
                  "k": [(fst[1].ap[:, 0:512], fst_hb[1][0]),
                        (fst[1].ap[:, 512:1024], fst_hb[1][1])],
                  "v": [(qstage.ap, qstage.buf), (kstage.ap, kstage.buf)]}
        for nm, srcd in (("q", wqT), ("k", wkT), ("v", wvT)):
            for h, (hap, hb) in enumerate(w_half[nm]):
                ap3 = hap.rearrange("p (a b) -> p a b", a=4)
                dma("sync", ap3,
                    srcd.rearrange("(c p) h -> p c h", p=128)[:, h * 4:h * 4 + 4, :],
                    [hb], [hb], f"w{nm}_in{h}")
        def emit_wsplits():
            for nm in ("q", "k", "v"):
                hi, lo = w_prj[nm]
                for h, (hap, hb) in enumerate(w_half[nm]):
                    ap3 = hap.rearrange("p (a b) -> p a b", a=4)
                    cs = slice(h * 4, h * 4 + 4)
                    act(hi.ap[:, cs, :], ap3, Act.Copy, [hb], [hi.buf],
                        tag=f"w{nm}_hi{h}")
                    vop(t_sub(lo.ap[:, cs, :], ap3, hi.ap[:, cs, :]),
                        [hb, hi.buf], [lo.buf], f"w{nm}_lo{h}")
            wstg1v = fst[0].ap
            dma("sync", wstg1v, woT, fst_hb[0], fst_hb[0], "wo_in")
            act(wo_hi.ap, wstg1v, Act.Copy, fst_hb[0], [wo_hi.buf])
            vop(t_sub(wo_lo.ap, wstg1v, wo_hi.ap), fst_hb[0] + [wo_hi.buf],
                [wo_lo.buf])
        def emit_wsums():
            for pj, nm in enumerate(("q", "k", "v")):
                w_h, w_l = w_prj[nm]
                i_mm = 0
                for wv in (w_h, w_l):
                    for ch in range(NCH):
                        mm(psum["pm0"].ap[0:1, 0:H], ones_m1.ap[:, 0:1],
                           wv.ap[:, ch, :], [ones_m1.buf, wv.buf],
                           [psum["pm0"].buf], i_mm == 0, i_mm == 15,
                           f"wsum_{nm}")
                        i_mm += 1
                act(wsumn.ap[0:1, pj * H:(pj + 1) * H], psum["pm0"].ap[0:1, 0:H],
                    Act.Identity, [psum["pm0"].buf], [wsumn.buf], scale=-1.0,
                    tag=f"wsumn_{nm}")

        slabs = []
        for nm, src in (("r", frT), ("k", fkT), ("v", fvT)):
            for ch in range(NCH):
                slabs.append((nm, src, ch))
        slab_i = [0]

        def stats_mm(src3, src_bufs, p0, p1, tag):
            for ch in range(NCH):
                mm(p0.ap[0:1, :], ones_m1f.ap[:, 0:1], src3[:, ch, :],
                   [ones_m1f.buf, src_bufs[ch]], [p0.buf], ch == 0, ch == NCH - 1,
                   f"{tag}_sum{ch}")
            for ch in range(NCH):
                s = sq[ch % 2]
                act(s.ap, src3[:, ch, :], Act.Square, [src_bufs[ch]], [s.buf],
                    tag=f"{tag}_sq{ch}")
                mm(p1.ap[0:1, :], ones_m1f.ap[:, 0:1], s.ap,
                   [ones_m1f.buf, s.buf], [p1.buf], ch == 0, ch == NCH - 1,
                   f"{tag}_msq{ch}")

        def ln_stats_rows(p0, p1, r_ap, r_buf, mr_ap, mr_buf, tag, scratch):
            ra, rb, rn = scratch
            vop((lambda o, i: lambda e: e.tensor_scalar_mul(o, i, 1.0 / C))
                (ra.ap, p0.ap[0:1, :]), [p0.buf], [ra.buf], f"{tag}_mean")
            vop((lambda o, i: lambda e: e.tensor_scalar_mul(o, i, 1.0 / C))
                (rb.ap, p1.ap[0:1, :]), [p1.buf], [rb.buf], f"{tag}_msq")
            vop((lambda o, a, b_: lambda e: e.scalar_tensor_tensor(
                o, a, -1.0, b_, op0=Alu.mult, op1=Alu.mult))
                (rn.ap, ra.ap, ra.ap), [ra.buf], [rn.buf], f"{tag}_negm2")
            vop((lambda o, a, b_: lambda e: e.scalar_tensor_tensor(
                o, a, EPS, b_, op0=Alu.add, op1=Alu.add))
                (rb.ap, rb.ap, rn.ap), [rb.buf, rn.buf], [rb.buf], f"{tag}_vareps")
            act(rb.ap, rb.ap, Act.Ln, [rb.buf], [rb.buf], tag=f"{tag}_ln")
            act(r_ap, rb.ap, Act.Exp, [rb.buf], [r_buf], scale=-0.5,
                tag=f"{tag}_rstd")
            vop(t_mul(mr_ap, ra.ap, r_ap), [ra.buf, r_buf], [mr_buf], f"{tag}_mr")

        def bcast_pair(r_ap, r_buf, mr_ap, mr_buf, b0v, b1v, n, tag):
            mm(psum["pbc0"].ap[:, 0:n], ones_k1f.ap, r_ap, [ones_k1f.buf, r_buf],
               [psum["pbc0"].buf], True, True, f"{tag}_bcR")
            mm(psum["pbc1"].ap[:, 0:n], ones_k1f.ap, mr_ap, [ones_k1f.buf, mr_buf],
               [psum["pbc1"].buf], True, True, f"{tag}_bcMR")
            vop(t_copy(b0v.ap[:, 0:n], psum["pbc0"].ap[:, 0:n]), [psum["pbc0"].buf],
                [b0v.buf])
            vop(t_copy(b1v.ap[:, 0:n], psum["pbc1"].ap[:, 0:n]), [psum["pbc1"].buf],
                [b1v.buf])

        def p1_rows(tb):
            ln_stats_rows(psum["prow0"], psum["prow1"], rowR.ap, rowR.buf,
                          rowMR.ap, rowMR.buf, f"ln1_{tb}", (rowA, rowB, rowN1))

        def p1_bcast(tb):
            mm(psum["pbc0"].ap, ones_k1f.ap, rowR.ap, [ones_k1f.buf, rowR.buf],
               [psum["pbc0"].buf], True, True, f"bcR_{tb}")
            vop(t_copy(bc0.ap, psum["pbc0"].ap), [psum["pbc0"].buf], [bc0.buf],
                f"bcRev_{tb}")

        def xsplit(tb):
            x3 = x_ring[tb % 3]
            x3b = x_ring_b[tb % 3]
            act(xs_hi.ap, x3.ap, Act.Copy, x3b, [xs_hi.buf], tag=f"xhi_{tb}")
            vop(t_sub(xs_lo.ap, x3.ap, xs_hi.ap), x3b + [xs_hi.buf], [xs_lo.buf],
                f"xlo_{tb}")

        def p1_proj(tb, names):
            ts = tb * TB1
            for nm in names:
                pj = {"q": 0, "k": 1, "v": 2}[nm]
                w_h, w_l = w_prj[nm]
                pw = (psum["pm0"], psum["pm1"], psum["pa0"])[pj]
                i_mm = 0
                for ch in range(NCH):
                    for (wv, xv) in ((w_h, xs_hi), (w_h, xs_lo), (w_l, xs_hi)):
                        mm(pw.ap, wv.ap[:, ch, :], xv.ap[:, ch, :],
                           [wv.buf, xv.buf], [pw.buf],
                           i_mm == 0, False, f"{nm}mm_{tb}_{ch}")
                        i_mm += 1
                mm(pw.ap, wsumn.ap[0:1, pj * H:(pj + 1) * H], rowA.ap,
                   [wsumn.buf, rowA.buf], [pw.buf], False, True, f"{nm}corr_{tb}")
                bias = bqkv.ap[:, pj:pj + 1]
                if nm == "v":
                    vop(t_mul(v_f.ap, pw.ap, bc0.ap), [pw.buf, bc0.buf],
                        [v_f.buf], f"vnrm_{tb}")
                    act(v_f.ap, v_f.ap, Act.Identity, [v_f.buf, bqkv.buf],
                        [v_f.buf], bias=bias, tag=f"vevac_{tb}")
                    for j in range(4):
                        i = 4 * tb + j
                        s = vstg[j % 2]
                        pv = psum["pa0"] if j % 2 == 0 else psum["pa1"]
                        tr.op("tensor",
                              (lambda o, i_, idn: lambda e: e.transpose(o, i_, idn))
                              (pv.ap[:, 0:H], v_f.ap[:, j * H:(j + 1) * H],
                               ident_s.ap),
                              reads=[v_f.buf, ident_s.buf], writes=[pv.buf],
                              tag=f"vtp_{i}")
                        act(s.ap, pv.ap[:, 0:H], Act.Copy,
                            [pv.buf], [s.buf], tag=f"vtev_{i}")
                        act(vt_hi.ap[:, i * H:(i + 1) * H], s.ap, Act.Copy,
                            [s.buf], [vt_hi.buf], tag=f"vthi_{i}")
                        vop(t_sub(vt_lo.ap[:, i * H:(i + 1) * H], s.ap,
                                  vt_hi.ap[:, i * H:(i + 1) * H]),
                            [s.buf, vt_hi.buf], [vt_lo.buf], f"vtlo_{i}")
                elif nm == "k":
                    vop(t_mul(kstage.ap, pw.ap, bc0.ap), [pw.buf, bc0.buf],
                        [kstage.buf], f"knrm_{tb}")
                    act(k_hi.ap[:, ts:ts + 512], kstage.ap, Act.Identity,
                        [kstage.buf, bqkv.buf], [k_hi.buf], bias=bias,
                        tag=f"{nm}hi_{tb}")
                    vop((lambda o, i, s_, x: lambda e: e.scalar_tensor_tensor(
                        o, i, s_, x, op0=Alu.add, op1=Alu.subtract))
                        (k_lo.ap[:, ts:ts + 512], kstage.ap, bias,
                         k_hi.ap[:, ts:ts + 512]),
                        [kstage.buf, bqkv.buf, k_hi.buf], [k_lo.buf],
                        f"{nm}lo_{tb}")
                else:
                    qh, ql = q_ri[tb % 2]
                    vop(t_mul(qstage.ap, pw.ap, bc0.ap), [pw.buf, bc0.buf],
                        [qstage.buf], f"qnrm_{tb}")
                    act(qh.ap, qstage.ap, Act.Identity, [qstage.buf, bqkv.buf],
                        [qh.buf], bias=bias, tag=f"{nm}hi_{tb}")
                    vop((lambda o, i, s_, x: lambda e: e.scalar_tensor_tensor(
                        o, i, s_, x, op0=Alu.add, op1=Alu.subtract))
                        (ql.ap, qstage.ap, bias, qh.ap),
                        [qstage.buf, bqkv.buf, qh.buf], [ql.buf], f"{nm}lo_{tb}")

        def emit_slab(i):
            nm, srcd, ch = slabs[i]
            st, sh, sl = fst[i % 2], fsh[0], fsl[0]
            hi_d, lo_d = scr[nm]
            rows = slice(ch * 128, (ch + 1) * 128)
            hb = fst_hb[i % 2]
            dma("gpsimd", st.ap, srcd[rows, :], [st.buf] + hb, [st.buf] + hb,
                f"fsl_{nm}{ch}_in")
            vop(t_copy(sh.ap, st.ap), [st.buf], [sh.buf], f"fslh_{nm}{ch}",
                eng="gpsimd")
            vop(t_sub(sl.ap, st.ap, sh.ap), [st.buf, sh.buf], [sl.buf], f"fsll_{nm}{ch}",
                eng="gpsimd")
            dma("gpsimd", hi_d[rows, :], sh.ap, [sh.buf], [scr_b[nm][0]],
                f"fsl_{nm}{ch}_outh")
            dma("gpsimd", lo_d[rows, :], sl.ap, [sl.buf], [scr_b[nm][1]],
                f"fsl_{nm}{ch}_outl")

        def pump_slabs(n):
            while slab_i[0] < len(slabs) and n > 0:
                emit_slab(slab_i[0])
                slab_i[0] += 1
                n -= 1

        def ln2_stats_flush(qb):
            pts = qb * QB
            stats_mm(x2tb.ap, x2tb_b, psum["pbc0"], psum["pbc1"],
                     f"ln2_{qb}")
            ln_stats_rows(psum["pbc0"], psum["pbc1"],
                          r2row.ap[0:1, pts:pts + 512], r2row.buf,
                          mr2row.ap[0:1, pts:pts + 512], mr2row.buf, f"ln2_{qb}",
                          (rowA2, rowB2, rowN2))
            for ch in range(NCH):
                dma("sync", x2d_r[:, ch, pts:pts + 512], x2tb.ap[:, ch, :],
                    [x2tb_b[ch]], [x2d_b2[qb][ch]], f"x2out_{qb}_{ch}")

        def attention(qb):
            qs0 = qb * QB
            nkj = 4 * qb + 4
            qh, ql = q_ri[qb % 2]
            first_sum = [True]
            first_av = [True]

            def st_chain(kj):
                nlo = max(0, 128 * kj - qs0)
                sl = kj % ERING
                pst = (psum["pm0"], psum["pm1"], psum["pa0"], psum["pa1"],
                       psum["pbc0"], psum["pbc1"])[kj % 6]
                ef = ef32[kj % 3]
                prs = [(k_hi.buf, k_hi, qh.buf, qh), (k_hi.buf, k_hi, ql.buf, ql),
                       (k_lo.buf, k_lo, qh.buf, qh)]
                for i_p, (kb, kv, qb_, qv) in enumerate(prs):
                    mm(pst.ap[:, nlo:512], kv.ap[:, kj * H:(kj + 1) * H],
                       qv.ap[:, nlo:512],
                       [kb, qb_], [pst.buf], i_p == 0, i_p == 2, f"st_{qb}_{kj}")
                act(ef.ap[:, nlo:512], pst.ap[:, nlo:512], Act.Exp,
                    [pst.buf], [ef.buf], scale=SCALE, tag=f"exp_{qb}_{kj}")
                act(e_hi.ap[:, sl, nlo:512], ef.ap[:, nlo:512], Act.Copy,
                    [ef.buf], [e_hi_b[sl]], tag=f"ehi_{qb}_{kj}")
                vop(t_sub(e_lo.ap[:, sl, nlo:512], ef.ap[:, nlo:512],
                          e_hi.ap[:, sl, nlo:512]),
                    [ef.buf, e_hi_b[sl]], [e_lo_b[sl]], f"elo_{qb}_{kj}")
                if 128 * kj >= qs0:
                    for ev, eb in ((e_hi, e_hi_b[sl]), (e_lo, e_lo_b[sl])):
                        vop(t_mul(ev.ap[:, sl, nlo:nlo + 128],
                                  ev.ap[:, sl, nlo:nlo + 128], tri_s.ap),
                            [eb, tri_s.buf], [eb], f"mask_{qb}_{kj}")

            def sums_av(kj, last_kj):
                nlo = max(0, 128 * kj - qs0)
                sl = kj % ERING
                mm(psum["prow0"].ap[:, nlo:512], ones_sq.ap,
                   e_hi.ap[:, sl, nlo:512], [ones_sq.buf, e_hi_b[sl]],
                   [psum["prow0"].buf], first_sum[0], False, f"sum_{qb}_{kj}")
                first_sum[0] = False
                for vv, vb in ((vt_hi, vt_hi.buf), (vt_lo, vt_lo.buf)):
                    mm(psum["prow1"].ap[:, nlo:512], vv.ap[:, kj * H:(kj + 1) * H],
                       e_hi.ap[:, sl, nlo:512], [vb, e_hi_b[sl]], [psum["prow1"].buf],
                       first_av[0], False, f"av_{qb}_{kj}")
                    first_av[0] = False
                mm(psum["prow0"].ap[:, nlo:512], ones_sq.ap,
                   e_lo.ap[:, sl, nlo:512], [ones_sq.buf, e_lo_b[sl]],
                   [psum["prow0"].buf], False, last_kj, f"sum_{qb}_{kj}l")
                mm(psum["prow1"].ap[:, nlo:512], vt_hi.ap[:, kj * H:(kj + 1) * H],
                   e_lo.ap[:, sl, nlo:512], [vt_hi.buf, e_lo_b[sl]],
                   [psum["prow1"].buf], False, last_kj, f"av_{qb}_{kj}l")

            ahead = min(4, nkj - 1)
            for kj in range(nkj):
                st_chain(kj)
                if kj >= ahead:
                    sums_av(kj - ahead, False)
            for kj in range(nkj - ahead, nkj):
                sums_av(kj, kj == nkj - 1)
            act(stmp.ap, psum["prow0"].ap[0:1, :], Act.Ln,
                [psum["prow0"].buf], [stmp.buf], tag=f"recln_{qb}")
            act(rs_row.ap, stmp.ap, Act.Exp, [stmp.buf], [rs_row.buf], scale=-1.0,
                tag=f"recip_{qb}")
            mm(psum["pa0"].ap, ones_k1f.ap, rs_row.ap, [ones_k1f.buf, rs_row.buf],
               [psum["pa0"].buf], True, True, f"rsbc_{qb}")
            vop(t_copy(rsbc.ap, psum["pa0"].ap), [psum["pa0"].buf], [rsbc.buf],
                f"rsev_{qb}")
            vop(t_mul(of_blk.ap, psum["prow1"].ap, rsbc.ap),
                [psum["prow1"].buf, rsbc.buf], [of_blk.buf], f"onrm_{qb}")
            act(o_hi.ap, of_blk.ap, Act.Copy, [of_blk.buf], [o_hi.buf],
                tag=f"ohi_{qb}")
            vop(t_sub(o_lo.ap, of_blk.ap, o_hi.ap),
                [of_blk.buf, o_hi.buf], [o_lo.buf], f"olo_{qb}")

        def wo_block(qb):
            ts = qb * QB
            for dch in range(NCH):
                pw = (psum["pa0"], psum["pa1"], psum["pm0"],
                      psum["pm1"])[dch % 4]
                prs = [(wo_hi, o_hi), (wo_lo, o_hi), (wo_hi, o_lo)]
                for i_p, (wv, ov) in enumerate(prs):
                    mm(pw.ap, wv.ap[:, dch * H:(dch + 1) * H], ov.ap,
                       [wv.buf, ov.buf], [pw.buf], i_p == 0, i_p == 2, f"wo_{qb}_{dch}")
                vop((lambda o, i, s, x: lambda e: e.scalar_tensor_tensor(
                    o, i, s, x, op0=Alu.add, op1=Alu.add))
                    (x2tb.ap[:, dch, :], pw.ap, bo_s.ap[:, dch:dch + 1],
                     x_ring[qb % 3].ap[:, dch, :]),
                    [pw.buf, bo_s.buf, x_ring_b[qb % 3][dch]], [x2tb_b[dch]],
                    f"woev_{qb}_{dch}")

        xsplit(0)
        stats_mm(x_ring[0].ap, x_ring_b[0], psum["prow0"], psum["prow1"],
                 "ln1_0")
        p1_rows(0)
        emit_wsplits()
        emit_wsums()
        p1_bcast(0)
        for tb in range(NTB1):
            p1_proj(tb, ("q",))
            p1_proj(tb, ("k", "v"))
            attention(tb)
            if tb + 1 < NTB1:
                xsplit(tb + 1)
                stats_mm(x_ring[(tb + 1) % 3].ap, x_ring_b[(tb + 1) % 3],
                         psum["prow0"], psum["prow1"], f"ln1_{tb + 1}")
                p1_rows(tb + 1)
            wo_block(tb)
            ln2_stats_flush(tb)
            if tb + 1 < NTB1:
                p1_bcast(tb + 1)
            if tb + 3 < NTB1:
                ts3 = (tb + 3) * TB1
                for ch in range(NCH):
                    dma("sync", x_ring[(tb + 3) % 3].ap[:, ch, :],
                        xT_r[:, ch, ts3:ts3 + 512],
                        [x_ring_b[(tb + 3) % 3][ch]], [x_ring_b[(tb + 3) % 3][ch]],
                        f"x_in{tb + 3}_{ch}")
            pump_slabs(6)
        pump_slabs(len(slabs))

        if debug:
            dma("sync", dbg["d_khi"], k_hi.ap, [k_hi.buf], [], "dbg")
            dma("sync", dbg["d_klo"], k_lo.ap, [k_lo.buf], [], "dbg")
            dma("sync", dbg["d_r2"], r2row.ap, [r2row.buf], [], "dbg")
            dma("sync", dbg["d_mr2"], mr2row.ap, [mr2row.buf], [], "dbg")
        A.free(x2tb)
        A.free(xs_hi); A.free(xs_lo)
        A.free(bc0); A.free(qstage); A.free(kstage); A.free(wsumn)
        for v in fst + fsh + fsl:
            A.free(v)
        A.free(e_hi); A.free(e_lo)
        for v in ef32:
            A.free(v)
        A.free(of_blk); A.free(rsbc); A.free(rs_row)
        tr.barrier()

        for v in x_ring:
            A.free(v)
        A.free(o_hi); A.free(o_lo); A.free(wo_hi); A.free(wo_lo)
        for qv in q_ri:
            A.free(qv[0]); A.free(qv[1])
        for v in (k_hi, k_lo, v_f, vt_hi, vt_lo):
            A.free(v)
        A.free(rowA); A.free(rowB); A.free(rowR); A.free(rowMR)
        A.free(rowA2); A.free(rowB2); A.free(rowN2); A.free(rowN1)
        A.free(stmp)
        A.free(sq[0]); A.free(sq[1]); A.free(vstg[0]); A.free(vstg[1])
        A.free(ident_s); A.free(tri_s); A.free(ones_k1); A.free(ones_m1)
        A.free(ones_sq)
        x2l = [A.alloc(f"x2l{i}", (128, NCH, TBF), F32) for i in range(2)]
        x2l_b = [[Buf(f"x2l{i}c{c}") for c in range(NCH)] for i in range(2)]
        h2b = A.alloc("h2b", (128, NCH, TBF + 1), F32)
        h2b_b = [Buf(f"h2b{c}") for c in range(NCH)]
        dbf = A.alloc("dbf", (128, NCH, TBF), F32)
        xfa = [A.alloc(f"xfa{i}", (128, TBF), F32) for i in range(4)]
        krl = [xfa[0], xfa[1]]
        xr_hi = [A.alloc("xr_hi0", (128, NCH, TBF), BF16)]
        xr_lo = [A.alloc("xr_lo0", (128, NCH, TBF), BF16)]
        xk_hi = [A.alloc("xk_hi0", (128, NCH, TBF), BF16)]
        xk_lo = [A.alloc("xk_lo0", (128, NCH, TBF), BF16)]
        kc_hi = A.alloc("kc_hi", (128, NCH, TBF), BF16)
        kc_lo = A.alloc("kc_lo", (128, NCH, TBF), BF16)
        xr_hi_b = [Buf(f"xrh{c}") for c in range(NCH)]
        xr_lo_b = [Buf(f"xrl{c}") for c in range(NCH)]
        xk_hi_b = [Buf(f"xkh{c}") for c in range(NCH)]
        xk_lo_b = [Buf(f"xkl{c}") for c in range(NCH)]
        kc_hi_b = [Buf(f"kch{c}") for c in range(NCH)]
        kc_lo_b = [Buf(f"kcl{c}") for c in range(NCH)]
        dbf_b = [Buf(f"dbf{c}") for c in range(NCH)]
        rgate = A.alloc("rgate", (128, NCH, TBF), F32)
        bcr0 = A.alloc("bcr0", (128, TBF), F32)
        bcr1 = A.alloc("bcr1", (128, TBF), F32)
        vtmp = [A.alloc(f"vtm{i}", (128, TBF), F32) for i in range(2)]
        wbh = [A.alloc(f"wbh{i}", (128, NCH, H), BF16) for i in range(4)]
        wbl = [A.alloc(f"wbl{i}", (128, NCH, H), BF16) for i in range(4)]
        wsched = [(k_, t_, d_) for t_ in range(NTBF) for k_ in ("r", "k", "v")
                  for d_ in range(NCH)]

        def emit_wst(g):
            if g >= len(wsched):
                return
            stg, t_, d_ = wsched[g]
            hi_d, lo_d = scr[stg]
            wh, wl = wbh[g % 4], wbl[g % 4]
            cols = slice(d_ * H, (d_ + 1) * H)
            dma("sync", wh.ap, hi_d[:, cols].rearrange("(c p) d -> p c d", p=128),
                [wh.buf, scr_b[stg][0]], [wh.buf], f"wst_{stg}_{t_}_{d_}h")
            dma("sync", wl.ap, lo_d[:, cols].rearrange("(c p) d -> p c d", p=128),
                [wl.buf, scr_b[stg][1]], [wl.buf], f"wst_{stg}_{t_}_{d_}l")
        pbank6 = (psum["pm0"], psum["pm1"], psum["pa0"], psum["pa1"],
                  psum["prow0"], psum["prow1"])

        def ffn_stage(tbf, ts, stage, xin_hi, xin_hi_b, xin_lo, xin_lo_b,
                      kind, xcur, mix_pump=None):
            xcur_b = x2l_b[tbf % 2]
            base = {"r": 0, "k": 2, "v": 4}[kind]
            g0 = tbf * 24 + {"r": 0, "k": 8, "v": 16}[kind]
            for dch in range(NCH):
                g = g0 + dch
                wh, wl = wbh[g % 4], wbl[g % 4]
                emit_wst(g + 2)
                if mix_pump:
                    for _ in range(2):
                        if mix_pump:
                            emit_mix_item(mix_pump.pop(0))
                pw = pbank6[(base + dch) % 6]
                i_mm = 0
                for ch in range(NCH):
                    for (wv, xv, xb) in ((wh, xin_hi, xin_hi_b[ch]),
                                         (wh, xin_lo, xin_lo_b[ch]),
                                         (wl, xin_hi, xin_hi_b[ch])):
                        mm(pw.ap[:, 0:TBF], wv.ap[:, ch, :], xv.ap[:, ch, :],
                           [wv.buf, xb], [pw.buf], i_mm == 0, i_mm == NCH * 3 - 1,
                           f"f{stage}_{tbf}_{dch}_{ch}")
                        i_mm += 1
                if kind == "r":
                    act(rgate.ap[:, dch, :], pw.ap[:, 0:TBF], Act.Sigmoid,
                        [pw.buf], [rgate.buf], tag=f"sig_{tbf}_{dch}")
                elif kind == "k":
                    kr = krl[dch % 2]
                    act(kr.ap, pw.ap[:, 0:TBF], Act.Relu, [pw.buf], [kr.buf],
                        tag=f"relu_{tbf}_{dch}")
                    vop((lambda o, a: lambda e: e.tensor_mul(o, a, a))
                        (dbf.ap[:, dch, :], kr.ap), [kr.buf], [dbf_b[dch]],
                        f"ksq_{tbf}_{dch}")
                    act(kc_hi.ap[:, dch, :], dbf.ap[:, dch, :], Act.Copy,
                        [dbf_b[dch]], [kc_hi_b[dch]], tag=f"kchi_{tbf}_{dch}")
                    vop(t_sub(kc_lo.ap[:, dch, :], dbf.ap[:, dch, :],
                              kc_hi.ap[:, dch, :]),
                        [dbf_b[dch], kc_hi_b[dch]], [kc_lo_b[dch]],
                        f"kclo_{tbf}_{dch}")
                else:
                    vt = vtmp[dch % 2]
                    vop(t_mul(vt.ap, pw.ap[:, 0:TBF], rgate.ap[:, dch, :]),
                        [pw.buf, rgate.buf], [vt.buf], f"vmul_{tbf}_{dch}")
                    vop(t_add(xcur.ap[:, dch, :], xcur.ap[:, dch, :], vt.ap),
                        [xcur_b[dch], vt.buf], [xcur_b[dch]], f"yadd_{tbf}_{dch}",
                        eng="gpsimd")

        def p5_mix_head(tbf):
            ts = tbf * TBF
            if tbf > 0:
                vop(t_copy(h2b.ap[:, :, 0:1], h2b.ap[:, :, TBF:TBF + 1]),
                    h2b_b, h2b_b, f"h2bd_{tbf}")
            bcast_pair(r2row.ap[0:1, ts:ts + TBF], r2row.buf,
                       mr2row.ap[0:1, ts:ts + TBF], mr2row.buf, bcr0, bcr1, TBF,
                       f"bc2_{tbf}")

        def emit_mix_item(it, flip=False):
            ch, tbf = it
            ts = tbf * TBF
            xc = x2l[tbf % 2]
            xcb = x2l_b[tbf % 2][ch]
            hv = "gpsimd" if flip else "vector"
            lo_e = "vector" if flip else "gpsimd"
            dma("sync", xc.ap[:, ch, :], x2d_r[:, ch, ts:ts + TBF],
                [x2d_b2[tbf][ch], xcb], [xcb], f"x2in_{tbf}_{ch}")
            vop(t_mul(h2b.ap[:, ch, 1:TBF + 1], xc.ap[:, ch, :],
                      bcr0.ap[:, 0:TBF]),
                [xcb, bcr0.buf, h2b_b[ch]], [h2b_b[ch]], f"h2m_{tbf}_{ch}",
                eng=hv)
            vop(t_sub(h2b.ap[:, ch, 1:TBF + 1], h2b.ap[:, ch, 1:TBF + 1],
                      bcr1.ap[:, 0:TBF]),
                [h2b_b[ch], bcr1.buf], [h2b_b[ch]], f"h2s_{tbf}_{ch}", eng=hv)
            vop(t_sub(dbf.ap[:, ch, :], h2b.ap[:, ch, 1:TBF + 1],
                      h2b.ap[:, ch, 0:TBF]),
                [h2b_b[ch]], [dbf_b[ch]], f"dmix_{tbf}_{ch}", eng=hv)
            for mu_s, dsth, dsth_b, dstl, dstl_b, nm in (
                    (mur_s, xr_hi[0], xr_hi_b, xr_lo[0], xr_lo_b, "r"),
                    (muk_s, xk_hi[0], xk_hi_b, xk_lo[0], xk_lo_b, "k")):
                xf = xfa[(2 * ch + (0 if nm == "r" else 1)) % 4]
                vop((lambda o, i, s, x: lambda e: e.scalar_tensor_tensor(
                    o, i, s, x, op0=Alu.mult, op1=Alu.add))
                    (xf.ap, dbf.ap[:, ch, :], mu_s.ap[:, ch:ch + 1],
                     h2b.ap[:, ch, 0:TBF]),
                    [dbf_b[ch], mu_s.buf, h2b_b[ch]], [xf.buf], f"x{nm}_{tbf}_{ch}")
                act(dsth.ap[:, ch, :], xf.ap, Act.Copy, [xf.buf], [dsth_b[ch]],
                    tag=f"x{nm}h_{tbf}_{ch}")
                vop(t_sub(dstl.ap[:, ch, :], xf.ap, dsth.ap[:, ch, :]),
                    [xf.buf, dsth_b[ch]], [dstl_b[ch]], f"x{nm}l_{tbf}_{ch}",
                    eng=lo_e)

        def p5_mix_items(tbf):
            return [(ch, tbf) for ch in range(NCH)]

        vop((lambda a: lambda e: e.memset(a, 0.0))(h2b.ap[:, :, 0:1]),
            [], h2b_b, "h2z")
        emit_wst(0)
        emit_wst(1)
        p5_mix_head(0)
        for it in p5_mix_items(0):
            emit_mix_item(it)
        for tbf in range(NTBF):
            ts = tbf * TBF
            xc = x2l[tbf % 2]
            ffn_stage(tbf, ts, "r", xr_hi[0], xr_hi_b, xr_lo[0], xr_lo_b, "r", xc)
            ffn_stage(tbf, ts, "k", xk_hi[0], xk_hi_b, xk_lo[0], xk_lo_b, "k", xc)
            pump = None
            if tbf + 1 < NTBF:
                p5_mix_head(tbf + 1)
                pump = p5_mix_items(tbf + 1)
            ffn_stage(tbf, ts, "v", kc_hi, kc_hi_b, kc_lo, kc_lo_b, "v", xc,
                      mix_pump=pump)
            if debug and tbf == 0:
                dma("sync", dbg["d_rg"], rgate.ap.rearrange("p a b -> p (a b)"),
                    [rgate.buf], [], "dbg")
                dma("sync", dbg["d_kch"], kc_hi.ap.rearrange("p a b -> p (a b)"),
                    kc_hi_b, [], "dbg")
                dma("sync", dbg["d_kcl"], kc_lo.ap.rearrange("p a b -> p (a b)"),
                    kc_lo_b, [], "dbg")
                dma("sync", dbg["d_xkh"], xk_hi[0].ap.rearrange("p a b -> p (a b)"),
                    xk_hi_b, [], "dbg")
                dma("sync", dbg["d_xkl"], xk_lo[0].ap.rearrange("p a b -> p (a b)"),
                    xk_lo_b, [], "dbg")
            dma("gpsimd", yT_r[:, :, ts:ts + TBF], xc.ap, x2l_b[tbf % 2], [],
                f"yout_{tbf}")

        tr.barrier()
        tr.resolve()
        tr.emit(block, sems)
    return nc


_NC = {}


def _get_nc(debug=False):
    if debug not in _NC:
        _NC[debug] = build_program(debug=debug)
    return _NC[debug]


def make_in_maps(L0_attn_x, L0_attn_Wq, L0_attn_bq, L0_attn_Wk, L0_attn_bk,
                 L0_attn_Wv, L0_attn_bv, L0_attn_Wo, L0_attn_bo,
                 L0_ffn_W_r, L0_ffn_W_k, L0_ffn_W_v, L0_ffn_mu_r, L0_ffn_mu_k):
    x = np.asarray(L0_attn_x, dtype=np.float32)
    common = {
        "wqT": np.ascontiguousarray(np.asarray(L0_attn_Wq, dtype=np.float32).T),
        "wkT": np.ascontiguousarray(np.asarray(L0_attn_Wk, dtype=np.float32).T),
        "wvT": np.ascontiguousarray(np.asarray(L0_attn_Wv, dtype=np.float32).T),
        "bqkv": np.ascontiguousarray(np.stack(
            [np.asarray(L0_attn_bq, dtype=np.float32),
             np.asarray(L0_attn_bk, dtype=np.float32),
             np.asarray(L0_attn_bv, dtype=np.float32)], axis=1)),
        "woT": np.ascontiguousarray(np.asarray(L0_attn_Wo, dtype=np.float32).T),
        "bo": np.asarray(L0_attn_bo, dtype=np.float32),
        "frT": np.ascontiguousarray(np.asarray(L0_ffn_W_r, dtype=np.float32).T),
        "fkT": np.ascontiguousarray(np.asarray(L0_ffn_W_k, dtype=np.float32).T),
        "fvT": np.ascontiguousarray(np.asarray(L0_ffn_W_v, dtype=np.float32).T),
        "mur": np.asarray(L0_ffn_mu_r, dtype=np.float32),
        "muk": np.asarray(L0_ffn_mu_k, dtype=np.float32),
        "tri": np.triu(np.ones((H, H))).astype(ml_dtypes.bfloat16),
        "ident": np.eye(H, dtype=np.float32),
    }
    in_maps = []
    for b in range(8):
        m = dict(common)
        m["xT"] = np.ascontiguousarray(x[b].T)
        in_maps.append(m)
    return in_maps


def kernel(**inputs):
    nc = _get_nc(debug=False)
    in_maps = make_in_maps(**inputs)
    res = run_bass_kernel_spmd(nc, in_maps, core_ids=list(range(8)))
    y = np.stack([np.asarray(res.results[b]["yT"]).T for b in range(8)], axis=0)
    return np.ascontiguousarray(y, dtype=np.float32)


# version 45
# speedup vs baseline: 1.0565x; 1.0115x over previous
import contextlib
import numpy as np
import ml_dtypes
import concourse.bass as bass
import concourse.mybir as mybir
from concourse.bass_utils import run_bass_kernel_spmd

F32 = mybir.dt.float32
BF16 = mybir.dt.bfloat16
Alu = mybir.AluOpType
Act = mybir.ActivationFunctionType
Ax = mybir.AxisListType

C, T, H, NCH = 1024, 2048, 128, 8
TB1, NTB1 = 512, 4
QB, NQB = 512, 4
TBF, NTBF = 512, 4
KJ = 16
SCALE = float(np.float32(128.0) ** np.float32(-0.5))
EPS = 1e-5
ARENA_W = 52736

ENGS = ("sync", "scalar", "vector", "tensor", "gpsimd")
NQSLOT = 10


class Buf:
    __slots__ = ("name", "writer", "readers")

    def __init__(self, name):
        self.name = name
        self.writer = None
        self.readers = {}


class Op:
    __slots__ = ("eng", "seq", "fn", "is_dma", "deps", "needs_inc", "inc_val",
                 "tag", "qslot")


class Tracker:

    def __init__(self):
        self.ops = {e: [] for e in ENGS}
        self.last_real = {e: None for e in ENGS}
        self.dma_cnt = {e: 0 for e in ENGS}
        self.slot_last = {}

    def op(self, eng, fn, reads=(), writes=(), is_dma=False, tag=""):
        o = Op()
        o.eng, o.fn, o.is_dma, o.tag = eng, fn, is_dma, tag
        o.seq = len(self.ops[eng])
        o.needs_inc = False
        o.inc_val = None
        o.qslot = None
        if is_dma:
            idx = self.dma_cnt[eng]
            self.dma_cnt[eng] = idx + 1
            o.qslot = idx % NQSLOT
            o.inc_val = (idx // NQSLOT + 1) * 16
            o.needs_inc = True
        dep = set()
        for b in reads:
            w = b.writer
            if w is not None and (w.is_dma or w.eng != eng):
                dep.add(w)
            b.readers[(eng, is_dma, o.qslot)] = o
        for b in writes:
            w = b.writer
            if w is not None and (w.is_dma or w.eng != eng):
                dep.add(w)
            for r in b.readers.values():
                if r is not o and (r.is_dma or r.eng != eng):
                    dep.add(r)
            b.writer = o
            b.readers = {}
        if is_dma:
            prev = self.slot_last.get((eng, o.qslot))
            if prev is not None:
                dep.add(prev)
            self.slot_last[(eng, o.qslot)] = o
        o.deps = dep
        self.ops[eng].append(o)
        if fn is not None:
            self.last_real[eng] = o
        return o

    def barrier(self):
        lasts = [self.last_real[e] for e in ENGS if self.last_real[e] is not None]
        lasts += list(self.slot_last.values())
        for e in ENGS:
            o = Op()
            o.eng, o.fn, o.is_dma, o.tag = e, None, False, "barrier"
            o.seq = len(self.ops[e])
            o.needs_inc = False
            o.inc_val = None
            o.qslot = None
            o.deps = {l for l in lasts if l.eng != e or l.is_dma}
            self.ops[e].append(o)

    def resolve(self):
        for e in ENGS:
            for o in self.ops[e]:
                for d in o.deps:
                    d.needs_inc = True
        for e in ENGS:
            cnt = 0
            for o in self.ops[e]:
                if o.is_dma:
                    continue
                if o.needs_inc:
                    if o.fn is None:
                        raise RuntimeError(f"dep on pure-wait op {o.eng}#{o.seq}")
                    cnt += 1
                    o.inc_val = cnt
        return self

    def emit(self, block, sems):
        def mk(section_name):
            ops = self.ops[section_name]

            def body(engine):
                covered = {}
                for o in ops:
                    for d in sorted(o.deps,
                                    key=lambda d: (d.eng, d.is_dma, d.qslot or 0,
                                                   d.inc_val)):
                        key = (("q", d.eng, d.qslot) if d.is_dma else ("c", d.eng))
                        v = d.inc_val
                        assert v is not None, (d.eng, d.seq, d.tag)
                        if covered.get(key, 0) < v:
                            engine.wait_ge(sems[key], v)
                            covered[key] = v
                    if o.fn is None:
                        continue
                    inst = o.fn(engine)
                    if o.needs_inc:
                        key = (("q", o.eng, o.qslot) if o.is_dma else ("c", o.eng))
                        inst.then_inc(sems[key], 16 if o.is_dma else 1)
            return body

        for e in ENGS:
            getattr(block, e)(mk(e))


class View:
    __slots__ = ("ap", "buf", "off", "words")

    def __getitem__(self, idx):
        return self.ap[idx]


class Arena:
    def __init__(self, tensor, words):
        self.t = tensor
        self.free_list = [(0, words)]

    def alloc(self, name, shape, dtype):
        per_part = 1
        for s in shape[1:]:
            per_part *= s
        w = per_part if dtype == F32 else (per_part + 1) // 2
        w = (w + 1) // 2 * 2
        for i, (off, sz) in enumerate(self.free_list):
            if sz >= w:
                if sz == w:
                    self.free_list.pop(i)
                else:
                    self.free_list[i] = (off + w, sz - w)
                v = View()
                v.off, v.words = off, w
                ap = self.t[0:shape[0], off:off + w]
                if dtype == BF16:
                    ap = ap.bitcast(BF16)
                n_free = per_part
                if len(shape) == 2:
                    ap = ap[:, 0:shape[1]]
                elif len(shape) == 3:
                    ap = ap[:, 0:n_free].rearrange("p (a b) -> p a b", a=shape[1])
                else:
                    raise ValueError(shape)
                v.ap = ap
                v.buf = Buf(name)
                return v
        raise MemoryError(f"arena OOM for {name} ({w} words); free={self.free_list}")

    def free(self, v):
        fl = self.free_list
        fl.append((v.off, v.words))
        fl.sort()
        out = []
        for off, sz in fl:
            if out and out[-1][0] + out[-1][1] == off:
                out[-1] = (out[-1][0], out[-1][1] + sz)
            else:
                out.append((off, sz))
        self.free_list = out


def build_program(debug=False):
    nc = bass.Bass("TRN2", target_bir_lowering=False)

    xT = nc.dram_tensor("xT", [C, T], F32, kind="ExternalInput").ap()
    wqT = nc.dram_tensor("wqT", [C, H], F32, kind="ExternalInput").ap()
    wkT = nc.dram_tensor("wkT", [C, H], F32, kind="ExternalInput").ap()
    wvT = nc.dram_tensor("wvT", [C, H], F32, kind="ExternalInput").ap()
    bqkv_d = nc.dram_tensor("bqkv", [H, 3], F32, kind="ExternalInput").ap()
    woT = nc.dram_tensor("woT", [H, C], F32, kind="ExternalInput").ap()
    bo_d = nc.dram_tensor("bo", [C], F32, kind="ExternalInput").ap()
    frT = nc.dram_tensor("frT", [C, C], F32, kind="ExternalInput").ap()
    fkT = nc.dram_tensor("fkT", [C, C], F32, kind="ExternalInput").ap()
    fvT = nc.dram_tensor("fvT", [C, C], F32, kind="ExternalInput").ap()
    mur_d = nc.dram_tensor("mur", [C], F32, kind="ExternalInput").ap()
    muk_d = nc.dram_tensor("muk", [C], F32, kind="ExternalInput").ap()
    tri_d = nc.dram_tensor("tri", [H, H], BF16, kind="ExternalInput").ap()
    ident_d = nc.dram_tensor("ident", [H, H], F32, kind="ExternalInput").ap()
    yT = nc.dram_tensor("yT", [C, T], F32, kind="ExternalOutput").ap()

    scr = {}
    scr_b = {}
    for nm in ("r", "k", "v"):
        scr[nm] = (nc.dram_tensor(f"s_hi_{nm}", [C, C], BF16).ap(),
                   nc.dram_tensor(f"s_lo_{nm}", [C, C], BF16).ap())
        scr_b[nm] = (Buf(f"s_hi_{nm}"), Buf(f"s_lo_{nm}"))
    x2d = nc.dram_tensor("x2d", [C, T], F32).ap()
    x2d_b2 = [[Buf(f"x2d{q}c{c}") for c in range(NCH)] for q in range(4)]
    x2d_r = x2d.rearrange("(c p) t -> p c t", p=128)

    dbg = {}
    if debug:
        for nm, shp in (("d_qhi", [H, T]), ("d_qlo", [H, T]), ("d_khi", [H, T]),
                        ("d_klo", [H, T]), ("d_vf", [H, T]), ("d_mil", [1, QB]),
                        ("d_ohi", [H, T]), ("d_olo", [H, T]), ("d_x2", [C, T]),
                        ("d_r2", [1, T]), ("d_mr2", [1, T]), ("d_rg", [C, TBF]),
                        ("d_ehi", [H, KJ * QB]), ("d_elo", [H, KJ * QB]),
                        ("d_kch", [H, NCH * TBF]), ("d_kcl", [H, NCH * TBF]),
                        ("d_xkh", [H, NCH * TBF]), ("d_xkl", [H, NCH * TBF])):
            d = BF16 if nm in ("d_qhi", "d_qlo", "d_khi", "d_klo", "d_ohi",
                               "d_olo", "d_ehi", "d_elo", "d_kch", "d_kcl",
                               "d_xkh", "d_xkl") else F32
            dbg[nm] = nc.dram_tensor(nm, shp, d, kind="ExternalOutput").ap()

    tr = Tracker()
    xT_r = xT.rearrange("(c p) t -> p c t", p=128)
    yT_r = yT.rearrange("(c p) t -> p c t", p=128)

    with (
        nc.sbuf_tensor("arena", [128, ARENA_W], F32) as arena_t,
        nc.psum_tensor("pm0", [128, 512], F32) as pm0_t,
        nc.psum_tensor("pm1", [128, 512], F32) as pm1_t,
        nc.psum_tensor("pa0", [128, 512], F32) as pa0_t,
        nc.psum_tensor("pa1", [128, 512], F32) as pa1_t,
        nc.psum_tensor("prow0", [128, 512], F32) as prow0_t,
        nc.psum_tensor("prow1", [128, 512], F32) as prow1_t,
        nc.psum_tensor("pbc0", [128, 512], F32) as pbc0_t,
        nc.psum_tensor("pbc1", [128, 512], F32) as pbc1_t,
        contextlib.ExitStack() as _es,
        nc.Block() as block,
    ):
        sems = {}
        for e in ("sync", "scalar", "gpsimd"):
            for s in range(NQSLOT):
                sems[("q", e, s)] = _es.enter_context(nc.semaphore(f"q_{e}_{s}"))
        for e in ("scalar", "vector", "tensor", "gpsimd"):
            sems[("c", e)] = _es.enter_context(nc.semaphore(f"c_{e}"))

        A = Arena(arena_t, ARENA_W)
        psum = {}
        for nm, t in (("pm0", pm0_t), ("pm1", pm1_t), ("pa0", pa0_t), ("pa1", pa1_t),
                      ("prow0", prow0_t), ("prow1", prow1_t), ("pbc0", pbc0_t),
                      ("pbc1", pbc1_t)):
            v = View()
            v.ap, v.buf, v.off, v.words = t[:], Buf(nm), -1, 0
            psum[nm] = v

        x_ring = [A.alloc(f"x_ring{i}", (128, NCH, 512), F32) for i in range(3)]
        x_ring_b = [[Buf(f"x{i}c{c}") for c in range(NCH)] for i in range(3)]
        w_prj = {}
        for nm in ("q", "k", "v"):
            w_prj[nm] = (A.alloc(f"w{nm}_hi", (128, NCH, H), BF16),
                         A.alloc(f"w{nm}_lo", (128, NCH, H), BF16))
        wo_hi = A.alloc("wo_hi", (128, C), BF16)
        wo_lo = A.alloc("wo_lo", (128, C), BF16)
        bqkv = A.alloc("bqkv", (128, 4), F32)
        bo_s = A.alloc("bo_s", (128, NCH), F32)
        mur_s = A.alloc("mur_s", (128, NCH), F32)
        muk_s = A.alloc("muk_s", (128, NCH), F32)
        q_ri = [(A.alloc(f"qh{i}", (128, 512), BF16),
                 A.alloc(f"ql{i}", (128, 512), BF16)) for i in range(2)]
        k_hi = A.alloc("k_hi", (128, T), BF16)
        k_lo = A.alloc("k_lo", (128, T), BF16)
        v_f = A.alloc("v_f", (128, 512), F32)
        vt_hi = A.alloc("vt_hi", (128, T), BF16)
        vt_lo = A.alloc("vt_lo", (128, T), BF16)
        o_hi = A.alloc("o_hi", (128, 512), BF16)
        o_lo = A.alloc("o_lo", (128, 512), BF16)
        r2row = A.alloc("r2row", (1, T), F32)
        mr2row = A.alloc("mr2row", (1, T), F32)
        ERING = 6
        e_hi = A.alloc("e_hi", (128, ERING, QB), BF16)
        e_lo = A.alloc("e_lo", (128, ERING, QB), BF16)
        e_hi_b = [Buf(f"e_hi{j}") for j in range(ERING)]
        e_lo_b = [Buf(f"e_lo{j}") for j in range(ERING)]
        ef32 = [A.alloc(f"ef32{i}", (128, 512), F32) for i in range(3)]
        of_blk = A.alloc("of_blk", (128, 512), F32)
        rsbc = A.alloc("rsbc", (128, 512), F32)
        fst = [A.alloc(f"fst{i}", (128, C), F32) for i in range(2)]
        fsh = [A.alloc("fsh0", (128, C), BF16)]
        fsl = [A.alloc("fsl0", (128, C), BF16)]
        ident_s = A.alloc("ident_s", (128, H), F32)
        tri_s = A.alloc("tri_s", (128, H), BF16)
        ones_k1 = A.alloc("ones_k1", (1, 128), BF16)
        ones_k1f = A.alloc("ones_k1f", (1, 128), F32)
        ones_m1 = A.alloc("ones_m1", (128, 2), BF16)
        ones_sq = A.alloc("ones_sq", (128, H), BF16)
        ones_m1f = A.alloc("ones_m1f", (128, 2), F32)
        rowA = A.alloc("rowA", (1, 512), F32)
        rowB = A.alloc("rowB", (1, 512), F32)
        rowR = A.alloc("rowR", (1, 512), F32)
        rowMR = A.alloc("rowMR", (1, 512), F32)
        stmp = A.alloc("stmp", (1, 512), F32)
        rowA2 = A.alloc("rowA2", (1, 512), F32)
        rowB2 = A.alloc("rowB2", (1, 512), F32)
        rowN2 = A.alloc("rowN2", (1, 512), F32)
        rowN1 = A.alloc("rowN1", (1, 512), F32)
        rs_row = A.alloc("rs_row", (1, 512), F32)

        x2tb = A.alloc("x2tb", (128, NCH, 512), F32)
        x2tb_b = [Buf(f"x2tb{c}") for c in range(NCH)]
        xs_hi = A.alloc("xs_hi", (128, NCH, 512), BF16)
        xs_lo = A.alloc("xs_lo", (128, NCH, 512), BF16)
        bc0 = A.alloc("bc0", (128, 512), F32)
        qstage = A.alloc("qstage", (128, 512), F32)
        kstage = A.alloc("kstage", (128, 512), F32)
        sq = [A.alloc(f"sq{i}", (128, 512), F32) for i in range(2)]
        vstg = [A.alloc(f"vstg{i}", (128, H), F32) for i in range(2)]
        wsumn = A.alloc("wsumn", (1, 3 * H), F32)

        def dma(eng, out_ap, in_ap, reads, writes, tag="", slow=False):
            tr.op(eng, (lambda o, i, s: lambda e: e.dma_start(
                o, i, allow_slow_non_contiguous=s))(out_ap, in_ap, slow),
                  reads=reads, writes=writes, is_dma=True, tag=tag)

        def mm(out_ap, l_ap, r_ap, reads, writes, start, stop, tag=""):
            tr.op("tensor",
                  (lambda o, l, r, st, sp: lambda e: e.matmul(o, l, r, start=st, stop=sp))
                  (out_ap, l_ap, r_ap, start, stop),
                  reads=reads, writes=writes, tag=tag)

        def act(out_ap, in_ap, func, reads, writes, bias=None, scale=None, tag=""):
            def f(o, i, fu, b, s):
                def run(e):
                    kw = {}
                    if b is not None:
                        kw["bias"] = b
                    if s is not None:
                        kw["scale"] = s
                    return e.activation(o, i, fu, **kw)
                return run
            tr.op("scalar", f(out_ap, in_ap, func, bias, scale),
                  reads=reads, writes=writes, tag=tag)

        def vop(fn, reads, writes, tag="", eng="vector"):
            tr.op(eng, fn, reads=reads, writes=writes, tag=tag)

        def t_sub(o, a, b):
            return lambda e: e.tensor_sub(o, a, b)

        def t_mul(o, a, b):
            return lambda e: e.tensor_mul(o, a, b)

        def t_add(o, a, b):
            return lambda e: e.tensor_add(o, a, b)

        def t_copy(o, a):
            return lambda e: e.tensor_copy(o, a)

        for tb in range(3):
            ts = tb * TB1
            for ch in range(NCH):
                dma("sync", x_ring[tb].ap[:, ch, :], xT_r[:, ch, ts:ts + 512],
                    [x_ring_b[tb][ch]], [x_ring_b[tb][ch]], f"x_in{tb}_{ch}")
        dma("sync", tri_s.ap, tri_d, [], [tri_s.buf])
        dma("sync", ident_s.ap, ident_d, [], [ident_s.buf])
        dma("sync", bqkv.ap[:, 0:3], bqkv_d, [], [bqkv.buf])
        dma("sync", bo_s.ap, bo_d.rearrange("(c p) -> p c", p=128), [], [bo_s.buf],
            slow=True)
        dma("sync", mur_s.ap, mur_d.rearrange("(c p) -> p c", p=128), [], [mur_s.buf],
            slow=True)
        dma("sync", muk_s.ap, muk_d.rearrange("(c p) -> p c", p=128), [], [muk_s.buf],
            slow=True)
        for v, val in ((ones_k1, 1.0), (ones_k1f, 1.0), (ones_m1, 1.0),
                       (ones_m1f, 1.0), (ones_sq, 1.0)):
            vop((lambda a, c_: lambda e: e.memset(a, c_))(v.ap, val), [], [v.buf],
                "memset")

        fst_hb = [[Buf(f"fst{i}h{h}") for h in range(2)] for i in range(2)]
        w_half = {"q": [(fst[0].ap[:, 0:512], fst_hb[0][0]),
                        (fst[0].ap[:, 512:1024], fst_hb[0][1])],
                  "k": [(fst[1].ap[:, 0:512], fst_hb[1][0]),
                        (fst[1].ap[:, 512:1024], fst_hb[1][1])],
                  "v": [(qstage.ap, qstage.buf), (kstage.ap, kstage.buf)]}
        for nm, srcd in (("q", wqT), ("k", wkT), ("v", wvT)):
            for h, (hap, hb) in enumerate(w_half[nm]):
                ap3 = hap.rearrange("p (a b) -> p a b", a=4)
                dma("sync", ap3,
                    srcd.rearrange("(c p) h -> p c h", p=128)[:, h * 4:h * 4 + 4, :],
                    [hb], [hb], f"w{nm}_in{h}")
        def emit_wsplits():
            for nm in ("q", "k", "v"):
                hi, lo = w_prj[nm]
                for h, (hap, hb) in enumerate(w_half[nm]):
                    ap3 = hap.rearrange("p (a b) -> p a b", a=4)
                    cs = slice(h * 4, h * 4 + 4)
                    act(hi.ap[:, cs, :], ap3, Act.Copy, [hb], [hi.buf],
                        tag=f"w{nm}_hi{h}")
                    vop(t_sub(lo.ap[:, cs, :], ap3, hi.ap[:, cs, :]),
                        [hb, hi.buf], [lo.buf], f"w{nm}_lo{h}")
            wstg1v = fst[0].ap
            dma("sync", wstg1v, woT, fst_hb[0], fst_hb[0], "wo_in")
            act(wo_hi.ap, wstg1v, Act.Copy, fst_hb[0], [wo_hi.buf])
            vop(t_sub(wo_lo.ap, wstg1v, wo_hi.ap), fst_hb[0] + [wo_hi.buf],
                [wo_lo.buf])
        def emit_wsums():
            for pj, nm in enumerate(("q", "k", "v")):
                w_h, w_l = w_prj[nm]
                i_mm = 0
                for wv in (w_h, w_l):
                    for ch in range(NCH):
                        mm(psum["pm0"].ap[0:1, 0:H], ones_m1.ap[:, 0:1],
                           wv.ap[:, ch, :], [ones_m1.buf, wv.buf],
                           [psum["pm0"].buf], i_mm == 0, i_mm == 15,
                           f"wsum_{nm}")
                        i_mm += 1
                act(wsumn.ap[0:1, pj * H:(pj + 1) * H], psum["pm0"].ap[0:1, 0:H],
                    Act.Identity, [psum["pm0"].buf], [wsumn.buf], scale=-1.0,
                    tag=f"wsumn_{nm}")

        slabs = []
        for nm, src in (("r", frT), ("k", fkT), ("v", fvT)):
            for ch in range(NCH):
                slabs.append((nm, src, ch))
        slab_i = [0]

        def stats_mm(src3, src_bufs, p0, p1, tag):
            for ch in range(NCH):
                mm(p0.ap[0:1, :], ones_m1f.ap[:, 0:1], src3[:, ch, :],
                   [ones_m1f.buf, src_bufs[ch]], [p0.buf], ch == 0, ch == NCH - 1,
                   f"{tag}_sum{ch}")
            for ch in range(NCH):
                s = sq[ch % 2]
                act(s.ap, src3[:, ch, :], Act.Square, [src_bufs[ch]], [s.buf],
                    tag=f"{tag}_sq{ch}")
                mm(p1.ap[0:1, :], ones_m1f.ap[:, 0:1], s.ap,
                   [ones_m1f.buf, s.buf], [p1.buf], ch == 0, ch == NCH - 1,
                   f"{tag}_msq{ch}")

        def ln_stats_rows(p0, p1, r_ap, r_buf, mr_ap, mr_buf, tag, scratch):
            ra, rb, rn = scratch
            vop((lambda o, i: lambda e: e.tensor_scalar_mul(o, i, 1.0 / C))
                (ra.ap, p0.ap[0:1, :]), [p0.buf], [ra.buf], f"{tag}_mean")
            vop((lambda o, i: lambda e: e.tensor_scalar_mul(o, i, 1.0 / C))
                (rb.ap, p1.ap[0:1, :]), [p1.buf], [rb.buf], f"{tag}_msq")
            vop((lambda o, a, b_: lambda e: e.scalar_tensor_tensor(
                o, a, -1.0, b_, op0=Alu.mult, op1=Alu.mult))
                (rn.ap, ra.ap, ra.ap), [ra.buf], [rn.buf], f"{tag}_negm2")
            vop((lambda o, a, b_: lambda e: e.scalar_tensor_tensor(
                o, a, EPS, b_, op0=Alu.add, op1=Alu.add))
                (rb.ap, rb.ap, rn.ap), [rb.buf, rn.buf], [rb.buf], f"{tag}_vareps")
            act(rb.ap, rb.ap, Act.Ln, [rb.buf], [rb.buf], tag=f"{tag}_ln")
            act(r_ap, rb.ap, Act.Exp, [rb.buf], [r_buf], scale=-0.5,
                tag=f"{tag}_rstd")
            vop(t_mul(mr_ap, ra.ap, r_ap), [ra.buf, r_buf], [mr_buf], f"{tag}_mr")

        def bcast_pair(r_ap, r_buf, mr_ap, mr_buf, b0v, b1v, n, tag):
            mm(psum["pbc0"].ap[:, 0:n], ones_k1f.ap, r_ap, [ones_k1f.buf, r_buf],
               [psum["pbc0"].buf], True, True, f"{tag}_bcR")
            mm(psum["pbc1"].ap[:, 0:n], ones_k1f.ap, mr_ap, [ones_k1f.buf, mr_buf],
               [psum["pbc1"].buf], True, True, f"{tag}_bcMR")
            vop(t_copy(b0v.ap[:, 0:n], psum["pbc0"].ap[:, 0:n]), [psum["pbc0"].buf],
                [b0v.buf])
            vop(t_copy(b1v.ap[:, 0:n], psum["pbc1"].ap[:, 0:n]), [psum["pbc1"].buf],
                [b1v.buf])

        def p1_rows(tb):
            ln_stats_rows(psum["prow0"], psum["prow1"], rowR.ap, rowR.buf,
                          rowMR.ap, rowMR.buf, f"ln1_{tb}", (rowA, rowB, rowN1))

        def p1_bcast(tb):
            mm(psum["pbc0"].ap, ones_k1f.ap, rowR.ap, [ones_k1f.buf, rowR.buf],
               [psum["pbc0"].buf], True, True, f"bcR_{tb}")
            vop(t_copy(bc0.ap, psum["pbc0"].ap), [psum["pbc0"].buf], [bc0.buf],
                f"bcRev_{tb}")

        def xsplit(tb):
            x3 = x_ring[tb % 3]
            x3b = x_ring_b[tb % 3]
            act(xs_hi.ap, x3.ap, Act.Copy, x3b, [xs_hi.buf], tag=f"xhi_{tb}")
            vop(t_sub(xs_lo.ap, x3.ap, xs_hi.ap), x3b + [xs_hi.buf], [xs_lo.buf],
                f"xlo_{tb}")

        def p1_proj(tb, names):
            ts = tb * TB1
            for nm in names:
                pj = {"q": 0, "k": 1, "v": 2}[nm]
                w_h, w_l = w_prj[nm]
                pw = (psum["pm0"], psum["pm1"], psum["pa0"])[pj]
                i_mm = 0
                for ch in range(NCH):
                    for (wv, xv) in ((w_h, xs_hi), (w_h, xs_lo), (w_l, xs_hi)):
                        mm(pw.ap, wv.ap[:, ch, :], xv.ap[:, ch, :],
                           [wv.buf, xv.buf], [pw.buf],
                           i_mm == 0, False, f"{nm}mm_{tb}_{ch}")
                        i_mm += 1
                mm(pw.ap, wsumn.ap[0:1, pj * H:(pj + 1) * H], rowA.ap,
                   [wsumn.buf, rowA.buf], [pw.buf], False, True, f"{nm}corr_{tb}")
                bias = bqkv.ap[:, pj:pj + 1]
                if nm == "v":
                    vop(t_mul(v_f.ap, pw.ap, bc0.ap), [pw.buf, bc0.buf],
                        [v_f.buf], f"vnrm_{tb}")
                    act(v_f.ap, v_f.ap, Act.Identity, [v_f.buf, bqkv.buf],
                        [v_f.buf], bias=bias, tag=f"vevac_{tb}")
                    for j in range(4):
                        i = 4 * tb + j
                        s = vstg[j % 2]
                        pv = psum["pa0"] if j % 2 == 0 else psum["pa1"]
                        tr.op("tensor",
                              (lambda o, i_, idn: lambda e: e.transpose(o, i_, idn))
                              (pv.ap[:, 0:H], v_f.ap[:, j * H:(j + 1) * H],
                               ident_s.ap),
                              reads=[v_f.buf, ident_s.buf], writes=[pv.buf],
                              tag=f"vtp_{i}")
                        act(s.ap, pv.ap[:, 0:H], Act.Copy,
                            [pv.buf], [s.buf], tag=f"vtev_{i}")
                        act(vt_hi.ap[:, i * H:(i + 1) * H], s.ap, Act.Copy,
                            [s.buf], [vt_hi.buf], tag=f"vthi_{i}")
                        vop(t_sub(vt_lo.ap[:, i * H:(i + 1) * H], s.ap,
                                  vt_hi.ap[:, i * H:(i + 1) * H]),
                            [s.buf, vt_hi.buf], [vt_lo.buf], f"vtlo_{i}")
                elif nm == "k":
                    vop(t_mul(kstage.ap, pw.ap, bc0.ap), [pw.buf, bc0.buf],
                        [kstage.buf], f"knrm_{tb}")
                    act(k_hi.ap[:, ts:ts + 512], kstage.ap, Act.Identity,
                        [kstage.buf, bqkv.buf], [k_hi.buf], bias=bias,
                        tag=f"{nm}hi_{tb}")
                    vop((lambda o, i, s_, x: lambda e: e.scalar_tensor_tensor(
                        o, i, s_, x, op0=Alu.add, op1=Alu.subtract))
                        (k_lo.ap[:, ts:ts + 512], kstage.ap, bias,
                         k_hi.ap[:, ts:ts + 512]),
                        [kstage.buf, bqkv.buf, k_hi.buf], [k_lo.buf],
                        f"{nm}lo_{tb}")
                else:
                    qh, ql = q_ri[tb % 2]
                    vop(t_mul(qstage.ap, pw.ap, bc0.ap), [pw.buf, bc0.buf],
                        [qstage.buf], f"qnrm_{tb}")
                    act(qh.ap, qstage.ap, Act.Identity, [qstage.buf, bqkv.buf],
                        [qh.buf], bias=bias, tag=f"{nm}hi_{tb}")
                    vop((lambda o, i, s_, x: lambda e: e.scalar_tensor_tensor(
                        o, i, s_, x, op0=Alu.add, op1=Alu.subtract))
                        (ql.ap, qstage.ap, bias, qh.ap),
                        [qstage.buf, bqkv.buf, qh.buf], [ql.buf], f"{nm}lo_{tb}")

        def emit_slab(i):
            nm, srcd, ch = slabs[i]
            st, sh, sl = fst[i % 2], fsh[0], fsl[0]
            hi_d, lo_d = scr[nm]
            rows = slice(ch * 128, (ch + 1) * 128)
            hb = fst_hb[i % 2]
            dma("gpsimd", st.ap, srcd[rows, :], [st.buf] + hb, [st.buf] + hb,
                f"fsl_{nm}{ch}_in")
            vop(t_copy(sh.ap, st.ap), [st.buf], [sh.buf], f"fslh_{nm}{ch}",
                eng="gpsimd")
            vop(t_sub(sl.ap, st.ap, sh.ap), [st.buf, sh.buf], [sl.buf], f"fsll_{nm}{ch}",
                eng="gpsimd")
            dma("gpsimd", hi_d[rows, :], sh.ap, [sh.buf], [scr_b[nm][0]],
                f"fsl_{nm}{ch}_outh")
            dma("gpsimd", lo_d[rows, :], sl.ap, [sl.buf], [scr_b[nm][1]],
                f"fsl_{nm}{ch}_outl")

        def pump_slabs(n):
            while slab_i[0] < len(slabs) and n > 0:
                emit_slab(slab_i[0])
                slab_i[0] += 1
                n -= 1

        def ln2_stats_flush(qb):
            pts = qb * QB
            stats_mm(x2tb.ap, x2tb_b, psum["pbc0"], psum["pbc1"],
                     f"ln2_{qb}")
            ln_stats_rows(psum["pbc0"], psum["pbc1"],
                          r2row.ap[0:1, pts:pts + 512], r2row.buf,
                          mr2row.ap[0:1, pts:pts + 512], mr2row.buf, f"ln2_{qb}",
                          (rowA2, rowB2, rowN2))
            for ch in range(NCH):
                dma("sync", x2d_r[:, ch, pts:pts + 512], x2tb.ap[:, ch, :],
                    [x2tb_b[ch]], [x2d_b2[qb][ch]], f"x2out_{qb}_{ch}")

        def attention(qb):
            qs0 = qb * QB
            nkj = 4 * qb + 4
            qh, ql = q_ri[qb % 2]
            first_sum = [True]
            first_av = [True]

            def st_chain(kj):
                nlo = max(0, 128 * kj - qs0)
                sl = kj % ERING
                pst = (psum["pm0"], psum["pm1"], psum["pa0"], psum["pa1"],
                       psum["pbc0"], psum["pbc1"])[kj % 6]
                ef = ef32[kj % 3]
                prs = [(k_hi.buf, k_hi, qh.buf, qh), (k_hi.buf, k_hi, ql.buf, ql),
                       (k_lo.buf, k_lo, qh.buf, qh)]
                for i_p, (kb, kv, qb_, qv) in enumerate(prs):
                    mm(pst.ap[:, nlo:512], kv.ap[:, kj * H:(kj + 1) * H],
                       qv.ap[:, nlo:512],
                       [kb, qb_], [pst.buf], i_p == 0, i_p == 2, f"st_{qb}_{kj}")
                act(ef.ap[:, nlo:512], pst.ap[:, nlo:512], Act.Exp,
                    [pst.buf], [ef.buf], scale=SCALE, tag=f"exp_{qb}_{kj}")
                act(e_hi.ap[:, sl, nlo:512], ef.ap[:, nlo:512], Act.Copy,
                    [ef.buf], [e_hi_b[sl]], tag=f"ehi_{qb}_{kj}")
                vop(t_sub(e_lo.ap[:, sl, nlo:512], ef.ap[:, nlo:512],
                          e_hi.ap[:, sl, nlo:512]),
                    [ef.buf, e_hi_b[sl]], [e_lo_b[sl]], f"elo_{qb}_{kj}")
                if 128 * kj >= qs0:
                    for ev, eb in ((e_hi, e_hi_b[sl]), (e_lo, e_lo_b[sl])):
                        vop(t_mul(ev.ap[:, sl, nlo:nlo + 128],
                                  ev.ap[:, sl, nlo:nlo + 128], tri_s.ap),
                            [eb, tri_s.buf], [eb], f"mask_{qb}_{kj}")

            def sums_av(kj, last_kj):
                nlo = max(0, 128 * kj - qs0)
                sl = kj % ERING
                mm(psum["prow0"].ap[:, nlo:512], ones_sq.ap,
                   e_hi.ap[:, sl, nlo:512], [ones_sq.buf, e_hi_b[sl]],
                   [psum["prow0"].buf], first_sum[0], False, f"sum_{qb}_{kj}")
                first_sum[0] = False
                for vv, vb in ((vt_hi, vt_hi.buf), (vt_lo, vt_lo.buf)):
                    mm(psum["prow1"].ap[:, nlo:512], vv.ap[:, kj * H:(kj + 1) * H],
                       e_hi.ap[:, sl, nlo:512], [vb, e_hi_b[sl]], [psum["prow1"].buf],
                       first_av[0], False, f"av_{qb}_{kj}")
                    first_av[0] = False
                mm(psum["prow0"].ap[:, nlo:512], ones_sq.ap,
                   e_lo.ap[:, sl, nlo:512], [ones_sq.buf, e_lo_b[sl]],
                   [psum["prow0"].buf], False, last_kj, f"sum_{qb}_{kj}l")
                mm(psum["prow1"].ap[:, nlo:512], vt_hi.ap[:, kj * H:(kj + 1) * H],
                   e_lo.ap[:, sl, nlo:512], [vt_hi.buf, e_lo_b[sl]],
                   [psum["prow1"].buf], False, last_kj, f"av_{qb}_{kj}l")

            ahead = min(5, nkj - 1)
            for kj in range(nkj):
                st_chain(kj)
                if kj >= ahead:
                    sums_av(kj - ahead, False)
            for kj in range(nkj - ahead, nkj):
                sums_av(kj, kj == nkj - 1)
            act(stmp.ap, psum["prow0"].ap[0:1, :], Act.Ln,
                [psum["prow0"].buf], [stmp.buf], tag=f"recln_{qb}")
            act(rs_row.ap, stmp.ap, Act.Exp, [stmp.buf], [rs_row.buf], scale=-1.0,
                tag=f"recip_{qb}")
            mm(psum["pa0"].ap, ones_k1f.ap, rs_row.ap, [ones_k1f.buf, rs_row.buf],
               [psum["pa0"].buf], True, True, f"rsbc_{qb}")
            vop(t_copy(rsbc.ap, psum["pa0"].ap), [psum["pa0"].buf], [rsbc.buf],
                f"rsev_{qb}")
            vop(t_mul(of_blk.ap, psum["prow1"].ap, rsbc.ap),
                [psum["prow1"].buf, rsbc.buf], [of_blk.buf], f"onrm_{qb}")
            act(o_hi.ap, of_blk.ap, Act.Copy, [of_blk.buf], [o_hi.buf],
                tag=f"ohi_{qb}")
            vop(t_sub(o_lo.ap, of_blk.ap, o_hi.ap),
                [of_blk.buf, o_hi.buf], [o_lo.buf], f"olo_{qb}")

        def wo_block(qb):
            ts = qb * QB
            for dch in range(NCH):
                pw = (psum["pa0"], psum["pa1"], psum["pm0"],
                      psum["pm1"])[dch % 4]
                prs = [(wo_hi, o_hi), (wo_lo, o_hi), (wo_hi, o_lo)]
                for i_p, (wv, ov) in enumerate(prs):
                    mm(pw.ap, wv.ap[:, dch * H:(dch + 1) * H], ov.ap,
                       [wv.buf, ov.buf], [pw.buf], i_p == 0, i_p == 2, f"wo_{qb}_{dch}")
                vop((lambda o, i, s, x: lambda e: e.scalar_tensor_tensor(
                    o, i, s, x, op0=Alu.add, op1=Alu.add))
                    (x2tb.ap[:, dch, :], pw.ap, bo_s.ap[:, dch:dch + 1],
                     x_ring[qb % 3].ap[:, dch, :]),
                    [pw.buf, bo_s.buf, x_ring_b[qb % 3][dch]], [x2tb_b[dch]],
                    f"woev_{qb}_{dch}")

        xsplit(0)
        stats_mm(x_ring[0].ap, x_ring_b[0], psum["prow0"], psum["prow1"],
                 "ln1_0")
        p1_rows(0)
        emit_wsplits()
        emit_wsums()
        p1_bcast(0)
        for tb in range(NTB1):
            p1_proj(tb, ("q",))
            p1_proj(tb, ("k", "v"))
            attention(tb)
            if tb + 1 < NTB1:
                xsplit(tb + 1)
                stats_mm(x_ring[(tb + 1) % 3].ap, x_ring_b[(tb + 1) % 3],
                         psum["prow0"], psum["prow1"], f"ln1_{tb + 1}")
                p1_rows(tb + 1)
            wo_block(tb)
            ln2_stats_flush(tb)
            if tb + 1 < NTB1:
                p1_bcast(tb + 1)
            if tb + 3 < NTB1:
                ts3 = (tb + 3) * TB1
                for ch in range(NCH):
                    dma("sync", x_ring[(tb + 3) % 3].ap[:, ch, :],
                        xT_r[:, ch, ts3:ts3 + 512],
                        [x_ring_b[(tb + 3) % 3][ch]], [x_ring_b[(tb + 3) % 3][ch]],
                        f"x_in{tb + 3}_{ch}")
            pump_slabs(6)
        pump_slabs(len(slabs))

        if debug:
            dma("sync", dbg["d_khi"], k_hi.ap, [k_hi.buf], [], "dbg")
            dma("sync", dbg["d_klo"], k_lo.ap, [k_lo.buf], [], "dbg")
            dma("sync", dbg["d_r2"], r2row.ap, [r2row.buf], [], "dbg")
            dma("sync", dbg["d_mr2"], mr2row.ap, [mr2row.buf], [], "dbg")
        A.free(x2tb)
        A.free(xs_hi); A.free(xs_lo)
        A.free(bc0); A.free(qstage); A.free(kstage); A.free(wsumn)
        for v in fst + fsh + fsl:
            A.free(v)
        A.free(e_hi); A.free(e_lo)
        for v in ef32:
            A.free(v)
        A.free(of_blk); A.free(rsbc); A.free(rs_row)
        wbh = [A.alloc(f"wbh{i}", (128, NCH, H), BF16) for i in range(4)]
        wbl = [A.alloc(f"wbl{i}", (128, NCH, H), BF16) for i in range(4)]
        x2l0 = A.alloc("x2l0", (128, NCH, TBF), F32)
        x2l0_b = [Buf(f"x2l0c{c}") for c in range(NCH)]
        for g in range(2):
            hi_d, lo_d = scr["r"]
            cols = slice(g * H, (g + 1) * H)
            dma("sync", wbh[g].ap,
                hi_d[:, cols].rearrange("(c p) d -> p c d", p=128),
                [wbh[g].buf, scr_b["r"][0]], [wbh[g].buf], f"wst_r_0_{g}h")
            dma("sync", wbl[g].ap,
                lo_d[:, cols].rearrange("(c p) d -> p c d", p=128),
                [wbl[g].buf, scr_b["r"][1]], [wbl[g].buf], f"wst_r_0_{g}l")
        for ch in range(NCH):
            dma("sync", x2l0.ap[:, ch, :], x2d_r[:, ch, 0:TBF],
                [x2d_b2[0][ch], x2l0_b[ch]], [x2l0_b[ch]], f"x2in_0_{ch}")
        tr.barrier()

        for v in x_ring:
            A.free(v)
        A.free(o_hi); A.free(o_lo); A.free(wo_hi); A.free(wo_lo)
        for qv in q_ri:
            A.free(qv[0]); A.free(qv[1])
        for v in (k_hi, k_lo, v_f, vt_hi, vt_lo):
            A.free(v)
        A.free(rowA); A.free(rowB); A.free(rowR); A.free(rowMR)
        A.free(rowA2); A.free(rowB2); A.free(rowN2); A.free(rowN1)
        A.free(stmp)
        A.free(sq[0]); A.free(sq[1]); A.free(vstg[0]); A.free(vstg[1])
        A.free(ident_s); A.free(tri_s); A.free(ones_k1); A.free(ones_m1)
        A.free(ones_sq)
        x2l = [x2l0, A.alloc("x2l1", (128, NCH, TBF), F32)]
        x2l_b = [x2l0_b, [Buf(f"x2l1c{c}") for c in range(NCH)]]
        h2b = A.alloc("h2b", (128, NCH, TBF + 1), F32)
        h2b_b = [Buf(f"h2b{c}") for c in range(NCH)]
        dbf = A.alloc("dbf", (128, NCH, TBF), F32)
        xfa = [A.alloc(f"xfa{i}", (128, TBF), F32) for i in range(4)]
        krl = [xfa[0], xfa[1]]
        xr_hi = [A.alloc("xr_hi0", (128, NCH, TBF), BF16)]
        xr_lo = [A.alloc("xr_lo0", (128, NCH, TBF), BF16)]
        xk_hi = [A.alloc("xk_hi0", (128, NCH, TBF), BF16)]
        xk_lo = [A.alloc("xk_lo0", (128, NCH, TBF), BF16)]
        kc_hi = A.alloc("kc_hi", (128, NCH, TBF), BF16)
        kc_lo = A.alloc("kc_lo", (128, NCH, TBF), BF16)
        xr_hi_b = [Buf(f"xrh{c}") for c in range(NCH)]
        xr_lo_b = [Buf(f"xrl{c}") for c in range(NCH)]
        xk_hi_b = [Buf(f"xkh{c}") for c in range(NCH)]
        xk_lo_b = [Buf(f"xkl{c}") for c in range(NCH)]
        kc_hi_b = [Buf(f"kch{c}") for c in range(NCH)]
        kc_lo_b = [Buf(f"kcl{c}") for c in range(NCH)]
        dbf_b = [Buf(f"dbf{c}") for c in range(NCH)]
        rgate = A.alloc("rgate", (128, NCH, TBF), F32)
        bcr0 = A.alloc("bcr0", (128, TBF), F32)
        bcr1 = A.alloc("bcr1", (128, TBF), F32)
        vtmp = [A.alloc(f"vtm{i}", (128, TBF), F32) for i in range(2)]
        wsched = [(k_, t_, d_) for t_ in range(NTBF) for k_ in ("r", "k", "v")
                  for d_ in range(NCH)]

        def emit_wst(g):
            if g >= len(wsched):
                return
            stg, t_, d_ = wsched[g]
            hi_d, lo_d = scr[stg]
            wh, wl = wbh[g % 4], wbl[g % 4]
            cols = slice(d_ * H, (d_ + 1) * H)
            dma("sync", wh.ap, hi_d[:, cols].rearrange("(c p) d -> p c d", p=128),
                [wh.buf, scr_b[stg][0]], [wh.buf], f"wst_{stg}_{t_}_{d_}h")
            dma("sync", wl.ap, lo_d[:, cols].rearrange("(c p) d -> p c d", p=128),
                [wl.buf, scr_b[stg][1]], [wl.buf], f"wst_{stg}_{t_}_{d_}l")
        pbank6 = (psum["pm0"], psum["pm1"], psum["pa0"], psum["pa1"],
                  psum["prow0"], psum["prow1"])

        def ffn_stage(tbf, ts, stage, xin_hi, xin_hi_b, xin_lo, xin_lo_b,
                      kind, xcur, mix_pump=None):
            xcur_b = x2l_b[tbf % 2]
            base = {"r": 0, "k": 2, "v": 4}[kind]
            g0 = tbf * 24 + {"r": 0, "k": 8, "v": 16}[kind]
            for dch in range(NCH):
                g = g0 + dch
                wh, wl = wbh[g % 4], wbl[g % 4]
                emit_wst(g + 2)
                if mix_pump:
                    for _ in range(2):
                        if mix_pump:
                            emit_mix_item(mix_pump.pop(0))
                pw = pbank6[(base + dch) % 6]
                i_mm = 0
                for ch in range(NCH):
                    for (wv, xv, xb) in ((wh, xin_hi, xin_hi_b[ch]),
                                         (wh, xin_lo, xin_lo_b[ch]),
                                         (wl, xin_hi, xin_hi_b[ch])):
                        mm(pw.ap[:, 0:TBF], wv.ap[:, ch, :], xv.ap[:, ch, :],
                           [wv.buf, xb], [pw.buf], i_mm == 0, i_mm == NCH * 3 - 1,
                           f"f{stage}_{tbf}_{dch}_{ch}")
                        i_mm += 1
                if kind == "r":
                    act(rgate.ap[:, dch, :], pw.ap[:, 0:TBF], Act.Sigmoid,
                        [pw.buf], [rgate.buf], tag=f"sig_{tbf}_{dch}")
                elif kind == "k":
                    kr = krl[dch % 2]
                    act(kr.ap, pw.ap[:, 0:TBF], Act.Relu, [pw.buf], [kr.buf],
                        tag=f"relu_{tbf}_{dch}")
                    vop((lambda o, a: lambda e: e.tensor_mul(o, a, a))
                        (dbf.ap[:, dch, :], kr.ap), [kr.buf], [dbf_b[dch]],
                        f"ksq_{tbf}_{dch}")
                    act(kc_hi.ap[:, dch, :], dbf.ap[:, dch, :], Act.Copy,
                        [dbf_b[dch]], [kc_hi_b[dch]], tag=f"kchi_{tbf}_{dch}")
                    vop(t_sub(kc_lo.ap[:, dch, :], dbf.ap[:, dch, :],
                              kc_hi.ap[:, dch, :]),
                        [dbf_b[dch], kc_hi_b[dch]], [kc_lo_b[dch]],
                        f"kclo_{tbf}_{dch}")
                else:
                    vt = vtmp[dch % 2]
                    vop(t_mul(vt.ap, pw.ap[:, 0:TBF], rgate.ap[:, dch, :]),
                        [pw.buf, rgate.buf], [vt.buf], f"vmul_{tbf}_{dch}")
                    vop(t_add(xcur.ap[:, dch, :], xcur.ap[:, dch, :], vt.ap),
                        [xcur_b[dch], vt.buf], [xcur_b[dch]], f"yadd_{tbf}_{dch}",
                        eng="gpsimd")

        def p5_mix_head(tbf):
            ts = tbf * TBF
            if tbf > 0:
                vop(t_copy(h2b.ap[:, :, 0:1], h2b.ap[:, :, TBF:TBF + 1]),
                    h2b_b, h2b_b, f"h2bd_{tbf}")
            bcast_pair(r2row.ap[0:1, ts:ts + TBF], r2row.buf,
                       mr2row.ap[0:1, ts:ts + TBF], mr2row.buf, bcr0, bcr1, TBF,
                       f"bc2_{tbf}")

        def emit_mix_item(it, flip=False):
            ch, tbf = it
            ts = tbf * TBF
            xc = x2l[tbf % 2]
            xcb = x2l_b[tbf % 2][ch]
            hv = "gpsimd" if flip else "vector"
            lo_e = "vector" if flip else "gpsimd"
            if tbf > 0:
                dma("sync", xc.ap[:, ch, :], x2d_r[:, ch, ts:ts + TBF],
                    [x2d_b2[tbf][ch], xcb], [xcb], f"x2in_{tbf}_{ch}")
            vop(t_mul(h2b.ap[:, ch, 1:TBF + 1], xc.ap[:, ch, :],
                      bcr0.ap[:, 0:TBF]),
                [xcb, bcr0.buf, h2b_b[ch]], [h2b_b[ch]], f"h2m_{tbf}_{ch}",
                eng=hv)
            vop(t_sub(h2b.ap[:, ch, 1:TBF + 1], h2b.ap[:, ch, 1:TBF + 1],
                      bcr1.ap[:, 0:TBF]),
                [h2b_b[ch], bcr1.buf], [h2b_b[ch]], f"h2s_{tbf}_{ch}", eng=hv)
            vop(t_sub(dbf.ap[:, ch, :], h2b.ap[:, ch, 1:TBF + 1],
                      h2b.ap[:, ch, 0:TBF]),
                [h2b_b[ch]], [dbf_b[ch]], f"dmix_{tbf}_{ch}", eng=hv)
            for mu_s, dsth, dsth_b, dstl, dstl_b, nm in (
                    (mur_s, xr_hi[0], xr_hi_b, xr_lo[0], xr_lo_b, "r"),
                    (muk_s, xk_hi[0], xk_hi_b, xk_lo[0], xk_lo_b, "k")):
                xf = xfa[(2 * ch + (0 if nm == "r" else 1)) % 4]
                vop((lambda o, i, s, x: lambda e: e.scalar_tensor_tensor(
                    o, i, s, x, op0=Alu.mult, op1=Alu.add))
                    (xf.ap, dbf.ap[:, ch, :], mu_s.ap[:, ch:ch + 1],
                     h2b.ap[:, ch, 0:TBF]),
                    [dbf_b[ch], mu_s.buf, h2b_b[ch]], [xf.buf], f"x{nm}_{tbf}_{ch}")
                act(dsth.ap[:, ch, :], xf.ap, Act.Copy, [xf.buf], [dsth_b[ch]],
                    tag=f"x{nm}h_{tbf}_{ch}")
                vop(t_sub(dstl.ap[:, ch, :], xf.ap, dsth.ap[:, ch, :]),
                    [xf.buf, dsth_b[ch]], [dstl_b[ch]], f"x{nm}l_{tbf}_{ch}",
                    eng=lo_e)

        def p5_mix_items(tbf):
            return [(ch, tbf) for ch in range(NCH)]

        vop((lambda a: lambda e: e.memset(a, 0.0))(h2b.ap[:, :, 0:1]),
            [], h2b_b, "h2z")
        p5_mix_head(0)
        for it in p5_mix_items(0):
            emit_mix_item(it)
        for tbf in range(NTBF):
            ts = tbf * TBF
            xc = x2l[tbf % 2]
            ffn_stage(tbf, ts, "r", xr_hi[0], xr_hi_b, xr_lo[0], xr_lo_b, "r", xc)
            ffn_stage(tbf, ts, "k", xk_hi[0], xk_hi_b, xk_lo[0], xk_lo_b, "k", xc)
            pump = None
            if tbf + 1 < NTBF:
                p5_mix_head(tbf + 1)
                pump = p5_mix_items(tbf + 1)
            ffn_stage(tbf, ts, "v", kc_hi, kc_hi_b, kc_lo, kc_lo_b, "v", xc,
                      mix_pump=pump)
            if debug and tbf == 0:
                dma("sync", dbg["d_rg"], rgate.ap.rearrange("p a b -> p (a b)"),
                    [rgate.buf], [], "dbg")
                dma("sync", dbg["d_kch"], kc_hi.ap.rearrange("p a b -> p (a b)"),
                    kc_hi_b, [], "dbg")
                dma("sync", dbg["d_kcl"], kc_lo.ap.rearrange("p a b -> p (a b)"),
                    kc_lo_b, [], "dbg")
                dma("sync", dbg["d_xkh"], xk_hi[0].ap.rearrange("p a b -> p (a b)"),
                    xk_hi_b, [], "dbg")
                dma("sync", dbg["d_xkl"], xk_lo[0].ap.rearrange("p a b -> p (a b)"),
                    xk_lo_b, [], "dbg")
            dma("gpsimd", yT_r[:, :, ts:ts + TBF], xc.ap, x2l_b[tbf % 2], [],
                f"yout_{tbf}")

        tr.barrier()
        tr.resolve()
        tr.emit(block, sems)
    return nc


_NC = {}


def _get_nc(debug=False):
    if debug not in _NC:
        _NC[debug] = build_program(debug=debug)
    return _NC[debug]


def make_in_maps(L0_attn_x, L0_attn_Wq, L0_attn_bq, L0_attn_Wk, L0_attn_bk,
                 L0_attn_Wv, L0_attn_bv, L0_attn_Wo, L0_attn_bo,
                 L0_ffn_W_r, L0_ffn_W_k, L0_ffn_W_v, L0_ffn_mu_r, L0_ffn_mu_k):
    x = np.asarray(L0_attn_x, dtype=np.float32)
    common = {
        "wqT": np.ascontiguousarray(np.asarray(L0_attn_Wq, dtype=np.float32).T),
        "wkT": np.ascontiguousarray(np.asarray(L0_attn_Wk, dtype=np.float32).T),
        "wvT": np.ascontiguousarray(np.asarray(L0_attn_Wv, dtype=np.float32).T),
        "bqkv": np.ascontiguousarray(np.stack(
            [np.asarray(L0_attn_bq, dtype=np.float32),
             np.asarray(L0_attn_bk, dtype=np.float32),
             np.asarray(L0_attn_bv, dtype=np.float32)], axis=1)),
        "woT": np.ascontiguousarray(np.asarray(L0_attn_Wo, dtype=np.float32).T),
        "bo": np.asarray(L0_attn_bo, dtype=np.float32),
        "frT": np.ascontiguousarray(np.asarray(L0_ffn_W_r, dtype=np.float32).T),
        "fkT": np.ascontiguousarray(np.asarray(L0_ffn_W_k, dtype=np.float32).T),
        "fvT": np.ascontiguousarray(np.asarray(L0_ffn_W_v, dtype=np.float32).T),
        "mur": np.asarray(L0_ffn_mu_r, dtype=np.float32),
        "muk": np.asarray(L0_ffn_mu_k, dtype=np.float32),
        "tri": np.triu(np.ones((H, H))).astype(ml_dtypes.bfloat16),
        "ident": np.eye(H, dtype=np.float32),
    }
    in_maps = []
    for b in range(8):
        m = dict(common)
        m["xT"] = np.ascontiguousarray(x[b].T)
        in_maps.append(m)
    return in_maps


def kernel(**inputs):
    nc = _get_nc(debug=False)
    in_maps = make_in_maps(**inputs)
    res = run_bass_kernel_spmd(nc, in_maps, core_ids=list(range(8)))
    y = np.stack([np.asarray(res.results[b]["yT"]).T for b in range(8)], axis=0)
    return np.ascontiguousarray(y, dtype=np.float32)
